# version 18
# baseline (speedup 1.0000x reference)
from contextlib import ExitStack

import numpy as np
import concourse.bass as bass
import concourse.mybir as mybir
from concourse.bass_utils import run_bass_kernel_spmd

F32 = mybir.dt.float32
BF16 = mybir.dt.bfloat16
AF = mybir.ActivationFunctionType
ALU = mybir.AluOpType
AX = mybir.AxisListType.X

LN_EPS = 1e-5
ATTN_EPS = 1e-15
QUANT_EPS = 1e-8


def _slices(span, width=512):
    out, off = [], 0
    while off < span:
        w = min(width, span - off)
        out.append((off, w))
        off += w
    return out


class Params:
    def __init__(self, C=2048, T=2048, E=8192, n_cores=8):
        self.C, self.T, self.E, self.n_cores = C, T, E, n_cores
        self.cores_per_b = n_cores // 2
        self.CC = C // 128
        self.TC = T // 128
        self.ECH = E // 128
        self.G = C // 128
        self.HALF = T // 2
        self.CQ = max(T // 4, 128)
        self.OGW = min(512, C)
        self.NOG = C // self.OGW
        self.QG = self.OGW // 128
        self.RU = C // n_cores
        self.RD = E // n_cores
        self.QW = min(2048, max(E, C))
        self.EG = 2
        self.DG = 2
        self.EB = min(4, self.ECH)
        self.NRU = (self.RU + 127) // 128


class Sem:
    __slots__ = ("h", "n")

    def __init__(self, ctx, nc, name):
        self.h = ctx.enter_context(nc.semaphore(name))
        self.n = 0

    def inc(self, inst, k=1):
        inst.then_inc(self.h, k)
        self.n += k
        return self.n


class DS:
    __slots__ = ("s", "unit")

    def __init__(self, ctx, nc, name, unit=16):
        self.s = [Sem(ctx, nc, name + "a"), Sem(ctx, nc, name + "b")]
        self.unit = unit

    def base(self):
        return (self.s[0].n, self.s[1].n)

    def inc(self, inst, i, k=16):
        sem_ = self.s[i % 2]
        inst.then_inc(sem_.h, k)
        sem_.n += k

    def wait_item(self, eng, b, i):
        eng.wait_ge(self.s[i % 2].h, b[i % 2] + self.unit * (i // 2 + 1))

    def wait_prev(self, eng, b, i):
        if i >= 2:
            self.wait_item(eng, b, i - 2)
        elif b[i % 2] > 0:
            eng.wait_ge(self.s[i % 2].h, b[i % 2])

    def wait_all(self, eng):
        eng.wait_ge(self.s[0].h, self.s[0].n)
        eng.wait_ge(self.s[1].h, self.s[1].n)


def build_nc(p: Params):
    nc = bass.Bass("TRN2", target_bir_lowering=False)
    C, T, E = p.C, p.T, p.E
    CC, TC, ECH, G, HALF, CQ = p.CC, p.TC, p.ECH, p.G, p.HALF, p.CQ
    NQ = T // CQ

    x_in = nc.dram_tensor("x_fm", [C, T], F32, kind="ExternalInput").ap()
    wqkvT = nc.dram_tensor("wqkvT", [C, 3 * C], F32, kind="ExternalInput").ap()
    wprojT = nc.dram_tensor("wprojT", [C, C], F32, kind="ExternalInput").ap()
    bproj = nc.dram_tensor("bproj", [1, C], F32, kind="ExternalInput").ap()
    wup_sl = nc.dram_tensor("wup_sl", [p.RU, E], F32, kind="ExternalInput").ap()
    wgate_sl = nc.dram_tensor("wgate_sl", [p.RU, E], F32, kind="ExternalInput").ap()
    wdown_sl = nc.dram_tensor("wdown_sl", [p.RD, C], F32, kind="ExternalInput").ap()
    mask129 = nc.dram_tensor("mask129", [128, 129], F32, kind="ExternalInput").ap()
    id128 = nc.dram_tensor("id128", [128, 128], F32, kind="ExternalInput").ap()
    p4m = nc.dram_tensor("p4m", [4, 128], F32, kind="ExternalInput").ap()
    bm4 = nc.dram_tensor("bm4", [128, 4], F32, kind="ExternalInput").ap()

    out_fm = nc.dram_tensor("out_fm", [C, T], F32, kind="ExternalOutput").ap()

    q_sp = nc.dram_tensor("q_sp", [G * 128, T], BF16).ap()
    x2_sp = nc.dram_tensor("x2_sp", [C, T], F32).ap()
    y4_sp = nc.dram_tensor("y4_sp", [C, T], F32).ap()
    scl_in = nc.dram_tensor("scl_in", [1, 3], F32).ap()
    scl_out = nc.dram_tensor("scl_out", [1, 3], F32).ap()
    scl_bc = nc.dram_tensor("scl_bc", [1, 12], F32).ap()
    tern_u_sl = nc.dram_tensor("tern_u_sl", [p.RU, E], BF16).ap()
    tern_g_sl = nc.dram_tensor("tern_g_sl", [p.RU, E], BF16).ap()
    tern_d_sl = nc.dram_tensor("tern_d_sl", [p.RD, C], BF16).ap()
    tern_u = nc.dram_tensor("tern_u", [C, E], BF16).ap()
    tern_g = nc.dram_tensor("tern_g", [C, E], BF16).ap()
    tern_d = nc.dram_tensor("tern_d", [E, C], BF16).ap()
    vk_part = nc.dram_tensor("vk_part", [G * 128, 129], F32).ap()
    vk_red = nc.dram_tensor("vk_red", [G * 128, 129], F32).ap()
    NRW = 2 * p.NRU
    rsum_part = nc.dram_tensor("rsum_part", [NRW, E], F32).ap()
    rsum_red = nc.dram_tensor("rsum_red", [NRW, E], F32).ap()

    all_cores = [list(range(p.n_cores))]
    batch_groups = [list(range(p.cores_per_b)),
                    list(range(p.cores_per_b, p.n_cores))]

    tiles = []
    for widx, (src, dst, R, W_) in enumerate(
            ((wup_sl, tern_u_sl, p.RU, E),
             (wgate_sl, tern_g_sl, p.RU, E),
             (wdown_sl, tern_d_sl, p.RD, C))):
        r = 0
        while r < R:
            rows = min(128, R - r)
            for (c0, w) in _slices(W_, p.QW):
                tiles.append((src, dst, r, rows, c0, w, widx))
            r += rows
    NTL = len(tiles)

    with ExitStack() as top:
        _uid = [0]

        def sb(ctx, name, shape, dt):
            _uid[0] += 1
            return ctx.enter_context(
                nc.sbuf_tensor(f"{name}_{_uid[0]}", shape, dt))

        def psum(ctx, name, shape, dt=F32):
            _uid[0] += 1
            return ctx.enter_context(
                nc.psum_tensor(f"{name}_{_uid[0]}", shape, dt))

        def sem(name):
            return Sem(top, nc, name)

        def ds(name, unit=16):
            return DS(top, nc, name, unit)

        def reuse_wait(eng, s, base, i, unit=1, nbuf=2):
            if i >= nbuf:
                eng.wait_ge(s.h, base + unit * (i - nbuf + 1))
            elif base > 0:
                eng.wait_ge(s.h, base)

        ones_col_bf = sb(top, "ones_col_bf", [128, 1], BF16)
        ones_col_f = sb(top, "ones_col_f", [128, 1], F32)
        ones_row_f = sb(top, "ones_row_f", [1, 128], F32)
        ones_row_bf = sb(top, "ones_row_bf", [1, 512], BF16)
        mask129_s = sb(top, "mask129_s", [128, 129], F32)
        p4_bf = sb(top, "p4_bf", [4, 128], BF16)
        bm4_s = sb(top, "bm4_s", [128, 4], F32)
        sc_b = sb(top, "sc_b", [128, 12], F32)

        block = top.enter_context(nc.Block())

        d_sl1 = ds("d_sl1"); d_sl2 = ds("d_sl2"); d_tern = ds("d_tern")
        d_x1 = ds("d_x1"); d_x2l = ds("d_x2l")
        d_wq = ds("d_wq"); d_qsp = ds("d_qsp")
        d_wkv = ds("d_wkv", unit=32)
        d_qld = ds("d_qld"); d_wp = ds("d_wp"); d_xrl = ds("d_xrl")
        d_x2st = ds("d_x2st")
        d_x3l = ds("d_x3l"); d_rs = ds("d_rs"); d_wd2 = ds("d_wd2")
        d_wug = ds("d_wug", unit=32); d_wd = ds("d_wd"); d_y4st = ds("d_y4st")
        d_xfl = ds("d_xfl"); d_yfl = ds("d_yfl"); d_out = ds("d_out")

        m_const = sem("m_const"); m_gpc = sem("m_gpc"); m_gpio = sem("m_gpio")
        m_nst = sem("m_nst"); m_vkst = sem("m_vkst")
        cc1 = sem("cc1"); cc2 = sem("cc2"); cc3 = sem("cc3")
        m_sum = sem("m_sum"); m_sclmm = sem("m_sclmm"); m_sclev = sem("m_sclev")
        m_scrow = sem("m_scrow"); m_q = sem("m_q")
        m_xc = sem("m_xc"); m_xs = sem("m_xs"); m_st1 = sem("m_st1")
        m_n1 = sem("m_n1")
        m_qmm = sem("m_qmm"); m_qev = sem("m_qev")
        m_kvmm = sem("m_kvmm"); m_kev = sem("m_kev"); m_vev = sem("m_vev")
        m_vkmm = sem("m_vkmm"); m_vkms = sem("m_vkms"); m_vone = sem("m_vone")
        m_den = sem("m_den")
        m_bnum = sem("m_bnum"); m_bts = sem("m_bts"); m_brec = sem("m_brec")
        m_bdiv = sem("m_bdiv"); m_bnev = sem("m_bnev"); m_batn = sem("m_batn")
        m_pmm = sem("m_pmm"); m_pev = sem("m_pev")
        m_ybf = sem("m_ybf"); m_ysq = sem("m_ysq"); m_st2 = sem("m_st2")
        m_x2n = sem("m_x2n")
        m_x3c = sem("m_x3c"); m_x3s = sem("m_x3s"); m_st3 = sem("m_st3")
        m_upmm = sem("m_upmm"); m_gtmm = sem("m_gtmm"); m_silu = sem("m_silu")
        m_hid = sem("m_hid")
        m_ev = sem("m_ev")
        m_dnsl = sem("m_dnsl"); m_dnev = sem("m_dnev")
        m_y4c = sem("m_y4c"); m_y4s = sem("m_y4s"); m_st4 = sem("m_st4")
        m_on = sem("m_on"); m_onp = sem("m_onp")
        m_var = sem("m_var"); m_sqr = sem("m_sqr"); m_row = sem("m_row")
        m_bcpe = sem("m_bcpe"); m_bc = sem("m_bc")

        def emit_rows(Tspan, ps_sum, ps_sq, m_pe_stats, pe_target,
                      s_row, q_row, rr, nmrr, scratch=None):
            msq = nmrr if scratch is None else scratch

            @block.vector
            def _(vector):
                vector.wait_ge(m_pe_stats.h, pe_target)
                vector.tensor_copy(s_row[0:1, 0:Tspan], ps_sum[0:1, 0:Tspan])
                vector.tensor_copy(q_row[0:1, 0:Tspan], ps_sq[0:1, 0:Tspan])
                vector.drain()
                vector.tensor_scalar(out=s_row[0:1, 0:Tspan],
                                     in0=s_row[0:1, 0:Tspan],
                                     scalar1=1.0 / C, scalar2=None, op0=ALU.mult)
                vector.drain()
                vector.tensor_tensor(out=msq[0:1, 0:Tspan],
                                     in0=s_row[0:1, 0:Tspan],
                                     in1=s_row[0:1, 0:Tspan], op=ALU.mult)
                vector.drain()
                vector.scalar_tensor_tensor(
                    out=q_row[0:1, 0:Tspan], in0=q_row[0:1, 0:Tspan],
                    scalar=1.0 / C, in1=msq[0:1, 0:Tspan],
                    op0=ALU.mult, op1=ALU.subtract)
                vector.drain()
                ins = vector.tensor_scalar(out=q_row[0:1, 0:Tspan],
                                           in0=q_row[0:1, 0:Tspan],
                                           scalar1=LN_EPS, scalar2=None,
                                           op0=ALU.add)
                m_var.inc(ins, 1)
            vtar = m_var.n

            @block.scalar
            def _(scalar):
                scalar.wait_ge(m_var.h, vtar)
                ins = scalar.activation(q_row[0:1, 0:Tspan], q_row[0:1, 0:Tspan],
                                        AF.Sqrt)
                m_sqr.inc(ins, 1)
            star = m_sqr.n

            @block.vector
            def _(vector):
                vector.wait_ge(m_sqr.h, star)
                vector.reciprocal(out=rr[0:1, 0:Tspan], in_=q_row[0:1, 0:Tspan])
                vector.drain()
                ins = vector.scalar_tensor_tensor(
                    out=nmrr[0:1, 0:Tspan], in0=s_row[0:1, 0:Tspan], scalar=-1.0,
                    in1=rr[0:1, 0:Tspan], op0=ALU.mult, op1=ALU.mult)
                m_row.inc(ins, 1)
            return m_row.n

        def emit_bcast(Tspan, rr, nmrr, rbt, nmrbt, row_target,
                       ps0, ps1, extra_pe_waits=()):
            sls = _slices(Tspan)
            pe_base, ev_base = m_bcpe.n, m_bc.n

            @block.tensor
            def _(tensor):
                tensor.wait_ge(m_row.h, row_target)
                for (s_, v_) in extra_pe_waits:
                    tensor.wait_ge(s_.h, v_)
                for si, (off, w) in enumerate(sls):
                    if si >= 1:
                        tensor.wait_ge(m_bc.h, ev_base + si)
                    elif ev_base > 0:
                        tensor.wait_ge(m_bc.h, ev_base)
                    tensor.matmul(ps0[:, 0:w], ones_row_f[:],
                                  rr[0:1, off:off + w], start=True, stop=True)
                    ins = tensor.matmul(ps1[:, 0:w], ones_row_f[:],
                                        nmrr[0:1, off:off + w],
                                        start=True, stop=True)
                    m_bcpe.inc(ins, 1)

            @block.vector
            def _(vector):
                for si, (off, w) in enumerate(sls):
                    vector.wait_ge(m_bcpe.h, pe_base + si + 1)
                    vector.tensor_copy(rbt[:, off:off + w], ps0[:, 0:w])
                    ins = vector.tensor_copy(nmrbt[:, off:off + w], ps1[:, 0:w])
                    m_bc.inc(ins, 1)
            return m_bc.n

        xln_ctx = ExitStack()
        x_ln1 = sb(xln_ctx, "x_ln1", [128, CC * T], BF16)
        vk_sb = sb(xln_ctx, "vk_sb", [128, G * 129], F32)
        a3w = ExitStack()
        OGW, NOG, QG = p.OGW, p.NOG, p.QG
        wk_sl = [sb(a3w, "wk_sl0", [128, CC * OGW], BF16), None]
        wv_sl = [sb(a3w, "wv_sl0", [128, CC * OGW], BF16), None]

        def kv_load(gpsimd, og):
            ins = gpsimd.dma_start(
                out=wk_sl[og % 2][:].rearrange("p (cc o) -> p cc o", cc=CC),
                in_=wqkvT[:, C + og * OGW:C + (og + 1) * OGW].rearrange(
                    "(cc p) o -> p cc o", p=128))
            d_wkv.inc(ins, og)
            ins = gpsimd.dma_start(
                out=wv_sl[og % 2][:].rearrange("p (cc o) -> p cc o", cc=CC),
                in_=wqkvT[:, 2 * C + og * OGW:2 * C + (og + 1) * OGW].rearrange(
                    "(cc p) o -> p cc o", p=128))
            d_wkv.inc(ins, og)

        sctx = ExitStack()
        qst = [sb(sctx, f"qst{i}", [128, p.QW], F32) for i in range(2)]
        tst = [sb(sctx, f"tst{i}", [128, p.QW], BF16) for i in range(4)]
        rs_row = sb(sctx, "rs_row", [1, min(512, p.QW)], F32)
        sums_s = sb(sctx, "sums_s", [128, NTL], F32)
        scl_row = sb(sctx, "scl_row", [1, 3], F32)
        sc_row12 = sb(sctx, "sc_row12", [1, 12], F32)

        @block.sync
        def _(sync):
            for ap_, dst in ((mask129, mask129_s), (bm4, bm4_s)):
                sync.dma_start(out=dst[:], in_=ap_[:, :]).then_inc(m_const.h, 16)
                m_const.n += 16
            for i, (src, dst, r0, rows, c0, w, widx) in enumerate(tiles):
                reuse_wait(sync, m_sum, 0, i)
                ins = sync.dma_start(out=qst[i % 2][0:rows, 0:w],
                                     in_=src[r0:r0 + rows, c0:c0 + w])
                d_sl1.inc(ins, i)

        @block.gpsimd
        def _(gpsimd):
            gpsimd.dma_start(out=p4_bf[:], in_=p4m[:, :]).then_inc(m_gpc.h, 16)
            m_gpc.n += 16

        b_sl1 = ((0, 0))

        @block.vector
        def _(vector):
            vector.memset(ones_col_bf[:], 1.0)
            vector.memset(ones_col_f[:], 1.0)
            vector.memset(ones_row_f[:], 1.0)
            vector.memset(ones_row_bf[:], 1.0)
            for i, (src, dst, r0, rows, c0, w, widx) in enumerate(tiles):
                d_sl1.wait_item(vector, (0, 0), i)
                ins = vector.reduce_sum(out=sums_s[0:rows, i:i + 1],
                                        in_=qst[i % 2][0:rows, 0:w], axis=AX,
                                        apply_absolute_value=True)
                m_sum.inc(ins, 1)
        assert m_sum.n == NTL

        sclp = ExitStack()
        ps_scl = psum(sclp, "ps_scl", [1, 3])

        @block.tensor
        def _(tensor):
            tensor.wait_ge(m_sum.h, NTL)
            for widx in range(3):
                sel = [i for i, t_ in enumerate(tiles) if t_[6] == widx]
                for j, i in enumerate(sel):
                    rows = tiles[i][3]
                    ins = tensor.matmul(ps_scl[0:1, widx:widx + 1],
                                        sums_s[0:rows, i:i + 1],
                                        ones_col_f[0:rows, :],
                                        start=(j == 0), stop=(j == len(sel) - 1))
            m_sclmm.inc(ins, 1)

        @block.vector
        def _(vector):
            vector.wait_ge(m_sclmm.h, 1)
            ins = vector.tensor_copy(scl_row[0:1, 0:3], ps_scl[0:1, 0:3])
            m_sclev.inc(ins, 1)

        sclp.close()

        a1 = ExitStack()
        xst = [sb(a1, f"xst{i}", [128, T], F32) for i in range(2)]
        xbf = [sb(a1, f"xbf{i}", [128, T], BF16) for i in range(2)]
        xsq = [sb(a1, f"xsq{i}", [128, T], BF16) for i in range(2)]
        sum_row = sb(a1, "sum_row", [1, T], F32)
        sq_row = sb(a1, "sq_row", [1, T], F32)
        r_b = sb(a1, "r_b", [128, T], F32)
        nmr_b = sb(a1, "nmr_b", [128, T], F32)

        @block.sync
        def _(sync):
            for cc in range(CC):
                if cc >= 2:
                    sync.wait_ge(m_xc.h, cc - 1)
                    sync.wait_ge(m_xs.h, cc - 1)
                ins = sync.dma_start(out=xst[cc % 2][:],
                                     in_=x_in[cc * 128:(cc + 1) * 128, :])
                d_x1.inc(ins, cc)

        @block.vector
        def _(vector):
            for cc in range(CC):
                d_x1.wait_item(vector, (0, 0), cc)
                if cc >= 2:
                    vector.wait_ge(m_st1.h, cc - 1)
                ins = vector.tensor_copy(xbf[cc % 2][:], xst[cc % 2][:])
                m_xc.inc(ins, 1)

        @block.scalar
        def _(scalar):
            for cc in range(CC):
                d_x1.wait_item(scalar, (0, 0), cc)
                if cc >= 2:
                    scalar.wait_ge(m_st1.h, cc - 1)
                ins = scalar.activation(xsq[cc % 2][:], xst[cc % 2][:], AF.Square)
                m_xs.inc(ins, 1)

        a1p = ExitStack()
        ps_sum1 = psum(a1p, "ps_sum1", [1, T])
        ps_sq1 = psum(a1p, "ps_sq1", [1, T])

        @block.tensor
        def _(tensor):
            tensor.wait_ge(m_sclev.h, 1)
            for cc in range(CC):
                tensor.wait_ge(m_xc.h, cc + 1)
                tensor.wait_ge(m_xs.h, cc + 1)
                for (off, w) in _slices(T):
                    tensor.matmul(ps_sum1[0:1, off:off + w], ones_col_bf[:],
                                  xbf[cc % 2][:, off:off + w],
                                  start=(cc == 0), stop=(cc == CC - 1))
                    ins = tensor.matmul(ps_sq1[0:1, off:off + w], ones_col_bf[:],
                                        xsq[cc % 2][:, off:off + w],
                                        start=(cc == 0), stop=(cc == CC - 1))
                m_st1.inc(ins, 1)
        assert m_st1.n == CC

        r1tar = emit_rows(T, ps_sum1, ps_sq1, m_st1, CC, sum_row, sq_row,
                          sq_row, sum_row, scratch=r_b[0:1, :])

        @block.gpsimd
        def _(gpsimd):
            gpsimd.wait_ge(m_sclev.h, 1)
            gpsimd.dma_start(out=scl_in, in_=scl_row[0:1, :]).then_inc(m_gpio.h, 16)
            m_gpio.n += 16
            gpsimd.wait_ge(m_gpio.h, 16)
            ins = gpsimd.collective_compute(
                "AllReduce", ALU.add, replica_groups=all_cores,
                ins=[scl_in.opt()], outs=[scl_out.opt()])
            cc1.inc(ins, 1)
            gpsimd.wait_ge(cc1.h, 1)
            gpsimd.dma_start(out=scl_row[0:1, :], in_=scl_out).then_inc(m_gpio.h, 16)
            m_gpio.n += 16

        @block.vector
        def _(vector):
            vector.wait_ge(m_gpio.h, 32)
            inv_n = 1.0 / float(C * E)
            vector.memset(sc_row12[:], 0.0)
            vector.drain()
            vector.tensor_scalar(out=sc_row12[0:1, 0:3], in0=scl_row[0:1, 0:3],
                                 scalar1=inv_n, scalar2=QUANT_EPS,
                                 op0=ALU.mult, op1=ALU.add)
            vector.drain()
            vector.tensor_scalar(out=sc_row12[0:1, 4:7], in0=sc_row12[0:1, 0:3],
                                 scalar1=0.5, scalar2=None, op0=ALU.mult)
            ins = vector.tensor_scalar(out=sc_row12[0:1, 8:11],
                                       in0=sc_row12[0:1, 0:3],
                                       scalar1=-0.5, scalar2=None, op0=ALU.mult)
            m_scrow.inc(ins, 1)

        @block.gpsimd
        def _(gpsimd):
            gpsimd.wait_ge(m_scrow.h, 1)
            gpsimd.dma_start(out=scl_bc, in_=sc_row12[0:1, :]).then_inc(m_gpio.h, 16)
            m_gpio.n += 16
            gpsimd.wait_ge(m_gpio.h, 48)
            gpsimd.dma_start(out=sc_b[:],
                             in_=scl_bc.to_broadcast((128, 12))).then_inc(m_gpio.h, 16)
            m_gpio.n += 16

        a1p.close()
        a1b = ExitStack()
        ps_b0 = psum(a1b, "ps_b0", [128, 512])
        ps_b1 = psum(a1b, "ps_b1", [128, 512])
        bc1tar = emit_bcast(T, sq_row, sum_row, r_b, nmr_b, r1tar, ps_b0, ps_b1)

        @block.sync
        def _(sync):
            for cc in range(CC):
                if cc < 2:
                    sync.wait_ge(m_xc.h, CC)
                    sync.wait_ge(m_xs.h, CC)
                else:
                    sync.wait_ge(m_n1.h, cc - 1)
                ins = sync.dma_start(out=xst[cc % 2][:],
                                     in_=x_in[cc * 128:(cc + 1) * 128, :])
                d_x2l.inc(ins, cc)

        @block.vector
        def _(vector):
            vector.wait_ge(m_bc.h, bc1tar)
            for cc in range(CC):
                d_x2l.wait_item(vector, (0, 0), cc)
                vector.tensor_tensor(out=xst[cc % 2][:], in0=xst[cc % 2][:],
                                     in1=r_b[:], op=ALU.mult)
                vector.drain()
                ins = vector.tensor_tensor(out=x_ln1[:, cc * T:(cc + 1) * T],
                                           in0=xst[cc % 2][:], in1=nmr_b[:],
                                           op=ALU.add)
                m_n1.inc(ins, 1)
        assert m_n1.n == CC

        @block.sync
        def _(sync):
            for i, (src, dst, r0, rows, c0, w, widx) in enumerate(tiles):
                reuse_wait(sync, m_q, 0, i)
                ins = sync.dma_start(out=qst[i % 2][0:rows, 0:w],
                                     in_=src[r0:r0 + rows, c0:c0 + w])
                d_sl2.inc(ins, i)

        NUG = sum(1 for t_ in tiles if t_[6] < 2)
        assert tiles[NUG - 1][6] < 2 and tiles[NUG][6] == 2
        RSB = m_sclmm.n
        RSEB = m_sclev.n

        @block.vector
        def _(vector):
            vector.wait_ge(m_gpio.h, 64)
            for i, (src, dst, r0, rows, c0, w, widx) in enumerate(tiles):
                buf, tbuf = qst[i % 2], tst[i % 4]
                d_sl2.wait_item(vector, (0, 0), i)
                if i >= 4:
                    d_tern.wait_item(vector, (0, 0), i - 4)
                    vector.wait_ge(m_sclmm.h, RSB + min(NUG, i - 3))
                vector.tensor_scalar(out=tbuf[0:rows, 0:w],
                                     in0=buf[0:rows, 0:w],
                                     scalar1=sc_b[0:rows, 8 + widx:9 + widx],
                                     scalar2=None, op0=ALU.is_le)
                vector.drain()
                ins = vector.scalar_tensor_tensor(
                    out=tbuf[0:rows, 0:w], in0=buf[0:rows, 0:w],
                    scalar=sc_b[0:rows, 4 + widx:5 + widx],
                    in1=tbuf[0:rows, 0:w], op0=ALU.is_ge, op1=ALU.subtract)
                m_q.inc(ins, 1)

        a1b.close()
        a1.close()
        a2 = ExitStack()
        wq_sl = [sb(a2, f"wq_sl{i}", [128, CC * 128], BF16) for i in range(3)]
        q_bf = [sb(a2, f"q_bf{i}", [128, T], BF16) for i in range(2)]
        ps_q = [psum(a2, f"ps_q{i}", [128, HALF]) for i in range(2)]
        NSRS = p.QW // 512
        ps_rs = [psum(a2, f"ps_rs{i}", [1, 512]) for i in range(NSRS)]
        assert NUG <= CC

        @block.gpsimd
        def _(gpsimd):
            gpsimd.wait_ge(m_n1.h, CC)
            for oc in range(CC):
                reuse_wait(gpsimd, m_qmm, 0, oc, unit=2, nbuf=3)
                if oc >= 2:
                    d_wq.wait_item(gpsimd, (0, 0), oc - 2)
                ins = gpsimd.dma_start(
                    out=wq_sl[oc % 3][:].rearrange("p (cc o) -> p cc o", cc=CC),
                    in_=wqkvT[:, oc * 128:(oc + 1) * 128].rearrange(
                        "(cc p) o -> p cc o", p=128))
                d_wq.inc(ins, oc)
                if oc == CC // 2:
                    kv_load(gpsimd, 0)

        @block.tensor
        def _(tensor):
            tensor.wait_ge(m_n1.h, CC)
            for oc in range(CC):
                d_wq.wait_item(tensor, (0, 0), oc)
                for th in range(2):
                    if oc >= 1:
                        tensor.wait_ge(m_qev.h, 2 * (oc - 1) + th + 1)
                    for cc in range(CC):
                        for (off, w) in _slices(HALF):
                            ins = tensor.matmul(
                                ps_q[th][:, off:off + w],
                                wq_sl[oc % 3][:, cc * 128:(cc + 1) * 128],
                                x_ln1[:, cc * T + th * HALF + off:
                                      cc * T + th * HALF + off + w],
                                start=(cc == 0), stop=(cc == CC - 1))
                    m_qmm.inc(ins, 1)
                if oc < NUG:
                    rows_, w_ = tiles[oc][3], tiles[oc][5]
                    tensor.wait_ge(m_q.h, oc + 1)
                    if oc >= 1:
                        tensor.wait_ge(m_sclev.h, RSEB + oc)
                    for si, (off, w2) in enumerate(_slices(w_)):
                        ins = tensor.matmul(ps_rs[si][0:1, 0:w2],
                                            ones_col_bf[0:rows_, :],
                                            tst[oc % 4][0:rows_, off:off + w2],
                                            start=True, stop=True)
                    m_sclmm.inc(ins, 1)
        assert m_qmm.n == 2 * CC and m_sclmm.n == RSB + NUG

        @block.scalar
        def _(scalar):
            for oc in range(CC):
                d_qsp.wait_prev(scalar, (0, 0), oc)
                for th in range(2):
                    scalar.wait_ge(m_qmm.h, 2 * oc + th + 1)
                    ins = scalar.activation(
                        q_bf[oc % 2][:, th * HALF:(th + 1) * HALF],
                        ps_q[th][:, 0:HALF], AF.Relu)
                    m_qev.inc(ins, 1)
                scalar.drain()
                ins = scalar.dma_start(out=q_sp[oc * 128:(oc + 1) * 128, :],
                                       in_=q_bf[oc % 2][:])
                d_qsp.inc(ins, oc)
                if oc < NUG:
                    (src, dst, r0, rows, c0, w, widx) = tiles[oc]
                    scalar.wait_ge(m_q.h, oc + 1)
                    if oc >= 2:
                        d_tern.wait_item(scalar, (0, 0), oc - 2)
                    ins = scalar.dma_start(out=dst[r0:r0 + rows, c0:c0 + w],
                                           in_=tst[oc % 4][0:rows, 0:w])
                    d_tern.inc(ins, oc)
                    scalar.wait_ge(m_sclmm.h, RSB + oc + 1)
                    rrow = widx * p.NRU + r0 // 128
                    hw_ = min(512, w)
                    nh = (w + hw_ - 1) // hw_
                    sls = _slices(w)
                    sph = hw_ // 512
                    for ih in range(nh):
                        if oc * nh + ih >= 1:
                            d_rs.wait_item(scalar, (0, 0), oc * nh + ih - 1)
                        for sj in range(sph):
                            si = ih * sph + sj
                            off, w2 = sls[si]
                            ins = scalar.copy(
                                rs_row[0:1, sj * 512:sj * 512 + w2],
                                ps_rs[si][0:1, 0:w2])
                        if ih == nh - 1:
                            m_sclev.inc(ins, 1)
                        scalar.drain()
                        ins = scalar.dma_start(
                            out=rsum_part[rrow:rrow + 1,
                                          c0 + ih * hw_:c0 + ih * hw_ + hw_],
                            in_=rs_row[0:1, 0:hw_])
                        d_rs.inc(ins, oc * nh + ih)

        @block.scalar
        def _(scalar):
            for i in range(NUG, NTL):
                (src, dst, r0, rows, c0, w, widx) = tiles[i]
                scalar.wait_ge(m_q.h, i + 1)
                if i >= 2:
                    d_tern.wait_item(scalar, (0, 0), i - 2)
                ins = scalar.dma_start(out=dst[r0:r0 + rows, c0:c0 + w],
                                       in_=tst[i % 4][0:rows, 0:w])
                d_tern.inc(ins, i)

        a2.close()
        sctx.close()
        a3 = ExitStack()
        wk_sl[1] = sb(a3, "wk_sl1", [128, CC * OGW], BF16)
        wv_sl[1] = sb(a3, "wv_sl1", [128, CC * OGW], BF16)
        VW = QG * 129
        k_tm = sb(a3, "k_tm", [128, TC * OGW], BF16)
        v_tm = sb(a3, "v_tm", [128, TC * OGW], BF16)
        ps_k = [psum(a3, f"ps_k{i}", [128, 512]) for i in range(2)]
        ps_v = [psum(a3, f"ps_v{i}", [128, 512]) for i in range(2)]
        ps_vk = [psum(a3, f"ps_vk{i}", [128, 512]) for i in range(QG)]

        @block.tensor
        def _(tensor):
            tensor.wait_ge(m_qev.h, 2 * CC)
            tensor.wait_ge(m_sclev.h, RSEB + NUG)

            def vk_group(it):
                og_, t_ = it // TC, it % TC
                tensor.wait_ge(m_kev.h, it + 1)
                tensor.wait_ge(m_vev.h, it + 1)
                if og_ >= 1 and t_ == 0:
                    tensor.wait_ge(m_vkms.h, og_ * QG)
                for qg in range(QG):
                    kt = k_tm[:, t_ * OGW + qg * 128:t_ * OGW + (qg + 1) * 128]
                    tensor.matmul(
                        ps_vk[qg][:, 0:128], kt,
                        v_tm[:, t_ * OGW + qg * 128:t_ * OGW + (qg + 1) * 128],
                        start=(t_ == 0), stop=False)
                    ins = tensor.matmul(
                        ps_vk[qg][:, 128:129], kt, ones_col_bf[:],
                        start=False, stop=(t_ == TC - 1))
                m_vkmm.inc(ins, 1)

            for og in range(NOG):
                d_wkv.wait_item(tensor, (0, 0), og)
                for t in range(TC):
                    it = og * TC + t
                    if it >= 2:
                        tensor.wait_ge(m_kev.h, it - 1)
                        tensor.wait_ge(m_vev.h, it - 1)
                    for cc in range(CC):
                        for (off, w) in _slices(OGW):
                            tensor.matmul(
                                ps_k[t % 2][:, off:off + w],
                                x_ln1[:, cc * T + t * 128:cc * T + (t + 1) * 128],
                                wk_sl[og % 2][:, cc * OGW + off:cc * OGW + off + w],
                                start=(cc == 0), stop=(cc == CC - 1))
                            ins = tensor.matmul(
                                ps_v[t % 2][:, off:off + w],
                                x_ln1[:, cc * T + t * 128:cc * T + (t + 1) * 128],
                                wv_sl[og % 2][:, cc * OGW + off:cc * OGW + off + w],
                                start=(cc == 0), stop=(cc == CC - 1))
                    m_kvmm.inc(ins, 1)
                    if it >= 1:
                        vk_group(it - 1)
            vk_group(NOG * TC - 1)

        @block.scalar
        def _(scalar):
            d_rs.wait_all(scalar)
            for og in range(NOG):
                for t in range(TC):
                    scalar.wait_ge(m_kvmm.h, og * TC + t + 1)
                    ins = scalar.activation(k_tm[:, t * OGW:(t + 1) * OGW],
                                            ps_k[t % 2][:, 0:OGW], AF.Relu)
                    m_kev.inc(ins, 1)

        @block.vector
        def _(vector):
            vector.wait_ge(m_const.h, 32)
            d_qsp.wait_all(vector)
            d_tern.wait_all(vector)
            d_rs.wait_all(vector)

            def vk_evac(og):
                vector.wait_ge(m_vkmm.h, (og + 1) * TC)
                for qg in range(QG):
                    g = og * QG + qg
                    ins = vector.tensor_tensor(
                        out=vk_sb[:, g * 129:(g + 1) * 129],
                        in0=ps_vk[qg][:, 0:129], in1=mask129_s[:], op=ALU.mult)
                    m_vkms.inc(ins, 1)

            for og in range(NOG):
                for t in range(TC):
                    vector.wait_ge(m_kvmm.h, og * TC + t + 1)
                    ins = vector.tensor_copy(
                        v_tm[:, t * OGW:(t + 1) * OGW],
                        ps_v[t % 2][:, 0:OGW])
                    m_vev.inc(ins, 1)
                    if og >= 1 and t == 1:
                        vk_evac(og - 1)
            vk_evac(NOG - 1)

        @block.scalar
        def _(scalar):
            scalar.wait_ge(m_vkms.h, G)
            scalar.dma_start(
                out=vk_part.rearrange("(g p) e -> p g e", p=128),
                in_=vk_sb[:].rearrange("p (g e) -> p g e", g=G)).then_inc(m_vkst.h, 16)
            m_vkst.n += 16

        a3.close()
        a3w.close()
        xln_ctx.close()
        b = ExitStack()
        bias_bf = sb(b, "bias_bf", [1, C], BF16)
        nst = sb(b, "nst", [128, G * 129], BF16)
        den_st = sb(b, "den_st", [128, G * 4], BF16)
        qh = [sb(b, f"qh{i}", [128, HALF], BF16) for i in range(2)]
        num_sb = [sb(b, f"num_sb{i}", [128, HALF], F32) for i in range(2)]
        rec_f = [sb(b, f"rec_f{i}", [4, HALF], F32) for i in range(2)]
        rec_bf = [sb(b, f"rec_bf{i}", [4, HALF], BF16) for i in range(2)]
        attn_bf = sb(b, "attn_bf", [128, G * HALF], BF16)
        wp_sl = [sb(b, f"wp_sl{i}", [128, CC * 128], BF16) for i in range(2)]
        y_f = sb(b, "y_f", [128, CC * HALF], F32)
        y_bf = [sb(b, f"y_bf{i}", [128, HALF], BF16) for i in range(2)]
        y_sq = [sb(b, f"y_sq{i}", [128, HALF], BF16) for i in range(2)]
        sum2 = sb(b, "sum2", [1, HALF], F32)
        sq2 = sb(b, "sq2", [1, HALF], F32)
        r2_row = sb(b, "r2_row", [1, HALF], F32)
        nmr2_row = sb(b, "nmr2_row", [1, HALF], F32)
        r2_b = sb(b, "r2_b", [128, HALF], F32)
        nmr2_b = sb(b, "nmr2_b", [128, HALF], F32)
        xres = [sb(b, f"xres{i}", [128, HALF], F32) for i in range(2)]
        x2t = [sb(b, f"x2t{i}", [128, HALF], F32) for i in range(2)]

        b_wp0 = d_wp.base()

        @block.gpsimd
        def _(gpsimd):
            d_tern.wait_all(gpsimd)
            d_qsp.wait_all(gpsimd)
            d_rs.wait_all(gpsimd)

            def og_load(og):
                reuse_wait(gpsimd, m_kvmm, 0, og, unit=TC)
                kv_load(gpsimd, og)

            if NOG > 1:
                og_load(1)
            ins = gpsimd.collective_compute(
                "AllReduce", ALU.add, replica_groups=all_cores,
                ins=[rsum_part.opt()], outs=[rsum_red.opt()])
            cc2.inc(ins, 1)
            ins = gpsimd.collective_compute(
                "AllGather", ALU.bypass, replica_groups=all_cores,
                ins=[tern_u_sl.opt()], outs=[tern_u.opt()])
            cc2.inc(ins, 1)
            if NOG > 2:
                og_load(2)
            ins = gpsimd.collective_compute(
                "AllGather", ALU.bypass, replica_groups=all_cores,
                ins=[tern_g_sl.opt()], outs=[tern_g.opt()])
            cc2.inc(ins, 1)
            for og in range(3, NOG):
                og_load(og)
            gpsimd.wait_ge(m_vkst.h, 16)
            ins = gpsimd.collective_compute(
                "AllReduce", ALU.add, replica_groups=batch_groups,
                ins=[vk_part.opt()], outs=[vk_red.opt()])
            cc3.inc(ins, 1)
            gpsimd.dma_start(out=bias_bf[:],
                             in_=bproj[:, :]).then_inc(m_gpc.h, 16)
            m_gpc.n += 16
            gpsimd.wait_ge(cc3.h, 1)
            gpsimd.dma_start(
                out=nst[:].rearrange("p (g e) -> p g e", g=G),
                in_=vk_red.rearrange("(g p) e -> p g e", p=128)).then_inc(
                    m_nst.h, 16)
            m_nst.n += 16
            for dc in range(2):
                ins = gpsimd.dma_start(
                    out=wp_sl[dc % 2][:].rearrange("p (cc o) -> p cc o", cc=CC),
                    in_=wprojT[:, dc * 128:(dc + 1) * 128].rearrange(
                        "(cc p) o -> p cc o", p=128))
                d_wp.inc(ins, dc)
            ins = gpsimd.collective_compute(
                "AllGather", ALU.bypass, replica_groups=all_cores,
                ins=[tern_d_sl.opt()], outs=[tern_d.opt()])
            cc2.inc(ins, 1)

        @block.vector
        def _(vector):
            vector.wait_ge(m_nst.h, 16)
            for g in range(G):
                ins = vector.tensor_tensor(
                    out=den_st[:, g * 4:(g + 1) * 4],
                    in0=nst[:, g * 129 + 128:g * 129 + 129].to_broadcast((128, 4)),
                    in1=bm4_s[:], op=ALU.mult)
            m_den.inc(ins, 1)

        for h in range(2):
            bp = ExitStack()
            ps_num = psum(bp, "ps_num", [128, HALF])
            ps_den = psum(bp, "ps_den", [4, HALF])
            ps_div = psum(bp, "ps_div", [128, HALF])

            b_qld = d_qld.base()
            NB, TB = m_bnum.n, m_bts.n
            RB, DB, EVB, AB = m_brec.n, m_bdiv.n, m_bnev.n, m_batn.n
            BCB = m_bc.n

            @block.sync
            def _(sync):
                if h == 0:
                    d_qsp.wait_all(sync)
                    sync.wait_ge(m_kvmm.h, NOG * TC)
                    sync.wait_ge(m_vkmm.h, NOG * TC)
                    sync.wait_ge(m_vkst.h, 16)
                for g in range(G):
                    reuse_wait(sync, m_bnum, NB, g)
                    ins = sync.dma_start(
                        out=qh[g % 2][:],
                        in_=q_sp[g * 128:(g + 1) * 128, h * HALF:(h + 1) * HALF])
                    d_qld.inc(ins, g)

            @block.tensor
            def _(tensor):
                tensor.wait_ge(m_den.h, 1)
                tensor.wait_ge(m_gpc.h, 32)
                if h == 1:
                    tensor.wait_ge(m_bc.h, BCB)

                def div_mm(j):
                    tensor.wait_ge(m_brec.h, RB + j + 1)
                    reuse_wait(tensor, m_batn, AB, j, nbuf=1)
                    for (off, w) in _slices(HALF):
                        ins = tensor.matmul(ps_div[:, off:off + w], p4_bf[:],
                                            rec_bf[j % 2][:, off:off + w],
                                            start=True, stop=True)
                    m_bdiv.inc(ins, 1)

                for g in range(G):
                    d_qld.wait_item(tensor, b_qld, g)
                    reuse_wait(tensor, m_bnev, EVB, g, nbuf=1)
                    reuse_wait(tensor, m_bts, TB, g, nbuf=1)
                    for (off, w) in _slices(HALF):
                        tensor.matmul(ps_num[:, off:off + w],
                                      nst[:, g * 129:g * 129 + 128],
                                      qh[g % 2][:, off:off + w],
                                      start=True, stop=True)
                        ins = tensor.matmul(ps_den[:, off:off + w],
                                            den_st[:, g * 4:(g + 1) * 4],
                                            qh[g % 2][:, off:off + w],
                                            start=True, stop=True)
                    m_bnum.inc(ins, 1)
                    if g >= 1:
                        div_mm(g - 1)
                div_mm(G - 1)

            @block.vector
            def _(vector):
                if h == 1:
                    vector.wait_ge(m_pmm.h, CC)
                for g in range(G):
                    vector.wait_ge(m_bnum.h, NB + g + 1)
                    reuse_wait(vector, m_bdiv, DB, g, nbuf=2)
                    vector.tensor_scalar(out=rec_f[g % 2][:, 0:HALF],
                                         in0=ps_den[:, 0:HALF],
                                         scalar1=ATTN_EPS, scalar2=None,
                                         op0=ALU.add)
                    ins = vector.drain()
                    m_bts.inc(ins, 1)
                    with nc.allow_low_precision(
                            reason="bf16 reciprocal feeds a bf16 matmul"):
                        ins = vector.reciprocal(out=rec_bf[g % 2][:, 0:HALF],
                                                in_=rec_f[g % 2][:, 0:HALF])
                    m_brec.inc(ins, 1)
                    vector.wait_ge(m_bnev.h, EVB + g + 1)
                    vector.wait_ge(m_bdiv.h, DB + g + 1)
                    ins = vector.tensor_tensor(
                        out=attn_bf[:, g * HALF:(g + 1) * HALF],
                        in0=num_sb[g % 2][:], in1=ps_div[:, 0:HALF], op=ALU.mult)
                    m_batn.inc(ins, 1)

            @block.scalar
            def _(scalar):
                for g in range(G):
                    scalar.wait_ge(m_bnum.h, NB + g + 1)
                    reuse_wait(scalar, m_batn, AB, g)
                    ins = scalar.copy(num_sb[g % 2][:], ps_num[:, 0:HALF])
                    m_bnev.inc(ins, 1)

            bp.close()

            b_wp = b_wp0 if h == 0 else d_wp.base()
            PB, PEB = m_pmm.n, m_pev.n
            YB, YS, SB2 = m_ybf.n, m_ysq.n, m_st2.n
            V2B = m_var.n

            @block.gpsimd
            def _(gpsimd):
                for dc in range(2 if h == 0 else 0, CC):
                    reuse_wait(gpsimd, m_pmm, PB, dc)
                    ins = gpsimd.dma_start(
                        out=wp_sl[dc % 2][:].rearrange("p (cc o) -> p cc o", cc=CC),
                        in_=wprojT[:, dc * 128:(dc + 1) * 128].rearrange(
                            "(cc p) o -> p cc o", p=128))
                    d_wp.inc(ins, dc)

            bps = ExitStack()
            ps_pj = [psum(bps, f"ps_pj{i}", [128, HALF]) for i in range(2)]
            ps_sum2 = psum(bps, "ps_sum2", [1, HALF])
            ps_sq2 = psum(bps, "ps_sq2", [1, HALF])

            def b_stats(tensor, dc):
                tensor.wait_ge(m_ybf.h, YB + dc + 1)
                tensor.wait_ge(m_ysq.h, YS + dc + 1)
                for (off, w) in _slices(HALF):
                    tensor.matmul(ps_sum2[0:1, off:off + w], ones_col_bf[:],
                                  y_bf[dc % 2][:, off:off + w],
                                  start=(dc == 0), stop=(dc == CC - 1))
                    ins = tensor.matmul(ps_sq2[0:1, off:off + w], ones_col_bf[:],
                                        y_sq[dc % 2][:, off:off + w],
                                        start=(dc == 0), stop=(dc == CC - 1))
                m_st2.inc(ins, 1)

            @block.tensor
            def _(tensor):
                tensor.wait_ge(m_batn.h, AB + G)
                if h == 1:
                    tensor.wait_ge(m_var.h, V2B)
                for dc in range(CC):
                    d_wp.wait_item(tensor, b_wp, dc)
                    reuse_wait(tensor, m_pev, PEB, dc)
                    for cc in range(CC):
                        for (off, w) in _slices(HALF):
                            tensor.matmul(
                                ps_pj[dc % 2][:, off:off + w],
                                wp_sl[dc % 2][:, cc * 128:(cc + 1) * 128],
                                attn_bf[:, cc * HALF + off:cc * HALF + off + w],
                                start=(cc == 0), stop=False)
                    for (off, w) in _slices(HALF):
                        ins = tensor.matmul(
                            ps_pj[dc % 2][:, off:off + w],
                            bias_bf[0:1, dc * 128:(dc + 1) * 128],
                            ones_row_bf[0:1, 0:w],
                            start=False, stop=True)
                    m_pmm.inc(ins, 1)
                    if dc >= 1:
                        b_stats(tensor, dc - 1)
                b_stats(tensor, CC - 1)

            @block.scalar
            def _(scalar):
                if h == 1:
                    scalar.wait_ge(m_x2n.h, CC)
                    scalar.wait_ge(m_ybf.h, CC)
                for dc in range(CC):
                    scalar.wait_ge(m_pmm.h, PB + dc + 1)
                    ins = scalar.copy(y_f[:, dc * HALF:(dc + 1) * HALF],
                                      ps_pj[dc % 2][:, 0:HALF])
                    m_pev.inc(ins, 1)
                    scalar.drain()
                    if dc >= 2:
                        scalar.wait_ge(m_st2.h, SB2 + dc - 1)
                    ins = scalar.activation(y_sq[dc % 2][:],
                                            y_f[:, dc * HALF:(dc + 1) * HALF],
                                            AF.Square)
                    m_ysq.inc(ins, 1)

            @block.vector
            def _(vector):
                for dc in range(CC):
                    vector.wait_ge(m_ysq.h, YS + dc + 1)
                    if dc >= 2:
                        vector.wait_ge(m_st2.h, SB2 + dc - 1)
                    ins = vector.tensor_copy(y_bf[dc % 2][:],
                                             y_f[:, dc * HALF:(dc + 1) * HALF])
                    m_ybf.inc(ins, 1)

            r2tar = emit_rows(HALF, ps_sum2, ps_sq2, m_st2, SB2 + CC,
                              sum2, sq2, r2_row, nmr2_row)
            bps.close()
            bbc = ExitStack()
            ps_c0 = psum(bbc, "ps_c0", [128, 512])
            ps_c1 = psum(bbc, "ps_c1", [128, 512])
            bc2tar = emit_bcast(HALF, r2_row, nmr2_row, r2_b, nmr2_b,
                                r2tar, ps_c0, ps_c1,
                                extra_pe_waits=((m_pev, PEB + CC),))

            b_xrl = d_xrl.base()
            b_x2st = d_x2st.base()
            NB2 = m_x2n.n

            @block.sync
            def _(sync):
                for cc in range(CC):
                    reuse_wait(sync, m_x2n, NB2, cc)
                    ins = sync.dma_start(
                        out=xres[cc % 2][:],
                        in_=x_in[cc * 128:(cc + 1) * 128, h * HALF:(h + 1) * HALF])
                    d_xrl.inc(ins, cc)

            @block.vector
            def _(vector):
                vector.wait_ge(m_bc.h, bc2tar)
                for cc in range(CC):
                    d_xrl.wait_item(vector, b_xrl, cc)
                    d_x2st.wait_prev(vector, b_x2st, cc)
                    vector.tensor_tensor(out=x2t[cc % 2][:],
                                         in0=y_f[:, cc * HALF:(cc + 1) * HALF],
                                         in1=r2_b[:], op=ALU.mult)
                    vector.drain()
                    vector.tensor_tensor(out=x2t[cc % 2][:], in0=x2t[cc % 2][:],
                                         in1=nmr2_b[:], op=ALU.add)
                    vector.drain()
                    ins = vector.tensor_tensor(out=x2t[cc % 2][:],
                                               in0=x2t[cc % 2][:],
                                               in1=xres[cc % 2][:], op=ALU.add)
                    m_x2n.inc(ins, 1)

            @block.scalar
            def _(scalar):
                for cc in range(CC):
                    scalar.wait_ge(m_x2n.h, NB2 + cc + 1)
                    ins = scalar.dma_start(
                        out=x2_sp[cc * 128:(cc + 1) * 128, h * HALF:(h + 1) * HALF],
                        in_=x2t[cc % 2][:])
                    d_x2st.inc(ins, cc)

            bbc.close()

        b.close()
        c = ExitStack()
        EG, DG, EBC = p.EG, p.DG, p.EB
        NEG = ECH // EG
        NDG = CC // DG
        NEB = ECH // EBC
        CPD = CC // NDG
        assert CPD * NDG == CC and NDG * DG == CC
        x2q_bf = [sb(c, f"x2q_bf{i}", [128, CC * CQ], BF16) for i in range(2)]
        x2f = [sb(c, f"x2f{i}", [128, CQ], F32) for i in range(2)]
        x2sq = [sb(c, f"x2sq{i}", [128, CQ], BF16) for i in range(2)]
        sum3 = sb(c, "sum3", [1, CQ], F32)
        sq3 = sb(c, "sq3", [1, CQ], F32)
        r3_row = sb(c, "r3_row", [1, CQ], F32)
        nmr3_row = sb(c, "nmr3_row", [1, CQ], F32)
        r3_b = sb(c, "r3_b", [128, CQ], F32)
        nmr3_b = sb(c, "nmr3_b", [128, CQ], F32)
        rsB = sb(c, "rsB", [ECH, 2 * p.NRU * 128], F32)
        id_s = sb(c, "id_s", [128, 128], F32)
        rsumT_u = sb(c, "rsumT_u", [128, ECH], F32)
        rsumT_g = sb(c, "rsumT_g", [128, ECH], F32)
        hid = sb(c, "hid", [128, ECH * CQ], BF16)
        wu_sl = [sb(c, f"wu_sl{i}", [128, CC * EG * 128], BF16) for i in range(2)]
        wg_sl = [sb(c, f"wg_sl{i}", [128, CC * EG * 128], BF16) for i in range(2)]
        au_t = [sb(c, f"au_t{i}", [128, CQ], F32) for i in range(2)]
        gt_t = [sb(c, f"gt_t{i}", [128, CQ], F32) for i in range(2)]
        silu_f = [sb(c, f"silu_f{i}", [128, CQ], F32) for i in range(2)]
        wd_sl = [sb(c, f"wd_sl{i}", [128, EBC * DG * 128], BF16) for i in range(4)]
        y4_f = [sb(c, f"y4_f{i}", [128, CQ], F32) for i in range(2)]
        y4bf = [sb(c, f"y4bf{i}", [128, CQ], BF16) for i in range(2)]
        y4sq = [sb(c, f"y4sq{i}", [128, CQ], BF16) for i in range(2)]
        sum4 = sb(c, "sum4", [1, CQ], F32)
        sq4 = sb(c, "sq4", [1, CQ], F32)
        r4_row = sb(c, "r4_row", [1, CQ], F32)
        nmr4_row = sb(c, "nmr4_row", [1, CQ], F32)
        r4_b = sb(c, "r4_b", [128, CQ], F32)
        nmr4_b = sb(c, "nmr4_b", [128, CQ], F32)
        x2st_ = [sb(c, f"x2st{i}", [128, CQ], F32) for i in range(3)]
        y4res = [sb(c, f"y4res{i}", [128, CQ], F32) for i in range(3)]
        outt = [sb(c, f"outt{i}", [128, CQ], F32) for i in range(2)]
        ps_up = [psum(c, f"ps_up{i}", [128, CQ]) for i in range(2)]
        ps_gt = [psum(c, f"ps_gt{i}", [128, CQ]) for i in range(2)]
        ps_dn = [psum(c, f"ps_dn{i}", [128, CQ]) for i in range(2)]
        ps_sum4 = psum(c, "ps_sum4", [1, CQ])
        ps_sq4 = psum(c, "ps_sq4", [1, CQ])

        NRW_ = 2 * p.NRU

        @block.gpsimd
        def _(gpsimd):
            gpsimd.wait_ge(cc2.h, 1)
            d_x2st.wait_all(gpsimd)
            ins = gpsimd.dma_start(out=id_s[:], in_=id128[:, :])
            m_gpio.inc(ins, 16)
            for rr_ in range(NRW_):
                ins = gpsimd.dma_start(
                    out=rsB[:, rr_ * 128:(rr_ + 1) * 128],
                    in_=rsum_red[rr_:rr_ + 1, :].rearrange(
                        "one (pp x) -> pp (one x)", pp=ECH))
                m_gpio.inc(ins, 16)
        RSLT = m_gpio.n

        @block.tensor
        def _(tensor):
            tensor.wait_ge(m_gpio.h, RSLT)
            tensor.wait_ge(m_bc.h, m_bc.n)
            for rr_ in range(NRW_):
                ins = tensor.matmul(ps_dn[0][:, rr_ * ECH:(rr_ + 1) * ECH],
                                    rsB[0:ECH, rr_ * 128:(rr_ + 1) * 128],
                                    id_s[0:ECH, 0:ECH],
                                    start=(rr_ == 0), stop=(rr_ == NRW_ - 1))
            m_sclmm.inc(ins, 1)
        RSMM_T = m_sclmm.n

        @block.vector
        def _(vector):
            vector.wait_ge(m_sclmm.h, RSMM_T)
            for (dst_, b0_) in ((rsumT_u, 0), (rsumT_g, p.NRU)):
                ins = vector.tensor_copy(
                    dst_[:], ps_dn[0][:, b0_ * ECH:(b0_ + 1) * ECH])
                if p.NRU > 1:
                    assert p.NRU == 2
                    vector.drain()
                    ins = vector.tensor_tensor(
                        out=dst_[:], in0=dst_[:],
                        in1=ps_dn[0][:, (b0_ + 1) * ECH:(b0_ + 2) * ECH],
                        op=ALU.add)
            m_sclev.inc(ins, 1)
            vector.drain()
        RSDONE = m_sclev.n

        b_x3l = d_x3l.base()
        XC0, XS0, ST3P = m_x3c.n, m_x3s.n, m_st3.n

        @block.gpsimd
        def _(gpsimd):
            for cc in range(CC):
                if cc >= 2:
                    gpsimd.wait_ge(m_x3c.h, XC0 + cc - 1)
                    gpsimd.wait_ge(m_x3s.h, XS0 + cc - 1)
                ins = gpsimd.dma_start(out=x2f[cc % 2][:],
                                       in_=x2_sp[cc * 128:(cc + 1) * 128,
                                                 0:CQ])
                d_x3l.inc(ins, cc)

        @block.vector
        def _(vector):
            for cc in range(CC):
                d_x3l.wait_item(vector, b_x3l, cc)
                ins = vector.tensor_copy(x2q_bf[0][:, cc * CQ:(cc + 1) * CQ],
                                         x2f[cc % 2][:])
                m_x3c.inc(ins, 1)

        @block.scalar
        def _(scalar):
            for cc in range(CC):
                d_x3l.wait_item(scalar, b_x3l, cc)
                reuse_wait(scalar, m_st3, ST3P, cc)
                ins = scalar.activation(x2sq[cc % 2][:], x2f[cc % 2][:],
                                        AF.Square)
                m_x3s.inc(ins, 1)

        @block.tensor
        def _(tensor):
            tensor.wait_ge(m_bc.h, m_bc.n)
            for cc in range(CC):
                tensor.wait_ge(m_x3c.h, XC0 + cc + 1)
                tensor.wait_ge(m_x3s.h, XS0 + cc + 1)
                for (off, w) in _slices(CQ):
                    tensor.matmul(ps_up[0][0:1, off:off + w], ones_col_bf[:],
                                  x2q_bf[0][:, cc * CQ + off:cc * CQ + off + w],
                                  start=(cc == 0), stop=(cc == CC - 1))
                    ins = tensor.matmul(ps_gt[0][0:1, off:off + w],
                                        ones_col_bf[:],
                                        x2sq[cc % 2][:, off:off + w],
                                        start=(cc == 0), stop=(cc == CC - 1))
                m_st3.inc(ins, 1)

        r3tar = emit_rows(CQ, ps_up[0], ps_gt[0], m_st3, ST3P + CC,
                          sum3, sq3, r3_row, nmr3_row)
        v3guard = m_var.n
        bc3tar = emit_bcast(CQ, r3_row, nmr3_row, r3_b, nmr3_b, r3tar,
                            ps_up[1], ps_gt[1])
        bc4tar_prev = m_bc.n

        for qq in range(NQ):
            q0 = qq * CQ
            last = (qq == NQ - 1)
            if qq == 0:
                b_wug = d_wug.base()
            UP, GT = m_upmm.n, m_gtmm.n
            SI, HD = m_silu.n, m_hid.n
            EV = m_ev.n

            def wug_load(gpsimd, eg):
                e0 = eg * EG * 128
                ew = EG * 128
                ins = gpsimd.dma_start(
                    out=wu_sl[eg % 2][:].rearrange("p (cc e) -> p cc e", cc=CC),
                    in_=tern_u[:, e0:e0 + ew].rearrange(
                        "(cc p) e -> p cc e", p=128))
                d_wug.inc(ins, eg)
                ins = gpsimd.dma_start(
                    out=wg_sl[eg % 2][:].rearrange("p (cc e) -> p cc e", cc=CC),
                    in_=tern_g[:, e0:e0 + ew].rearrange(
                        "(cc p) e -> p cc e", p=128))
                d_wug.inc(ins, eg)

            @block.gpsimd
            def _(gpsimd):
                if qq == 0:
                    gpsimd.wait_ge(cc2.h, 4)
                    d_x2st.wait_all(gpsimd)
                for eg in range(min(2, NEG) if qq >= 1 else 0, NEG):
                    reuse_wait(gpsimd, m_gtmm, GT, eg, unit=EG)
                    wug_load(gpsimd, eg)

            @block.tensor
            def _(tensor):
                tensor.wait_ge(m_x3c.h, m_x3c.n)
                tensor.wait_ge(m_var.h, v3guard)
                for ec in range(ECH):
                    eg, el = ec // EG, ec % EG
                    d_wug.wait_item(tensor, b_wug, eg)
                    if ec == 1:
                        tensor.wait_ge(m_bc.h, bc3tar)
                    if ec >= 2:
                        tensor.wait_ge(m_ev.h, EV + 3 * (ec - 1))
                    elif EV > 0:
                        tensor.wait_ge(m_ev.h, EV)
                    for cc in range(CC):
                        for (off, w) in _slices(CQ):
                            ins = tensor.matmul(
                                ps_up[ec % 2][:, off:off + w],
                                wu_sl[eg % 2][:, cc * EG * 128 + el * 128:
                                              cc * EG * 128 + (el + 1) * 128],
                                x2q_bf[qq % 2][:, cc * CQ + off:
                                               cc * CQ + off + w],
                                start=(cc == 0), stop=(cc == CC - 1))
                    m_upmm.inc(ins, 1)
                    for cc in range(CC):
                        for (off, w) in _slices(CQ):
                            ins = tensor.matmul(
                                ps_gt[ec % 2][:, off:off + w],
                                wg_sl[eg % 2][:, cc * EG * 128 + el * 128:
                                              cc * EG * 128 + (el + 1) * 128],
                                x2q_bf[qq % 2][:, cc * CQ + off:
                                               cc * CQ + off + w],
                                start=(cc == 0), stop=(cc == CC - 1))
                    m_gtmm.inc(ins, 1)

            @block.vector
            def _(vector):
                if qq >= 1:
                    vector.wait_ge(m_dnsl.h, qq * NDG * NEB)
                vector.wait_ge(m_bc.h, bc3tar)
                for ec in range(ECH):
                    vector.wait_ge(m_upmm.h, UP + ec + 1)
                    reuse_wait(vector, m_silu, SI, ec)
                    ins = vector.tensor_tensor(out=au_t[ec % 2][:],
                                               in0=ps_up[ec % 2][:, 0:CQ],
                                               in1=r3_b[:], op=ALU.mult)
                    m_ev.inc(ins, 1)
                    vector.drain()
                    ins = vector.scalar_tensor_tensor(
                        out=au_t[ec % 2][:], in0=nmr3_b[:],
                        scalar=rsumT_u[:, ec:ec + 1],
                        in1=au_t[ec % 2][:], op0=ALU.mult, op1=ALU.add)
                    m_ev.inc(ins, 1)
                    vector.wait_ge(m_gtmm.h, GT + ec + 1)
                    ins = vector.tensor_tensor(out=gt_t[ec % 2][:],
                                               in0=ps_gt[ec % 2][:, 0:CQ],
                                               in1=r3_b[:], op=ALU.mult)
                    m_ev.inc(ins, 1)
                    vector.drain()
                    vector.scalar_tensor_tensor(
                        out=gt_t[ec % 2][:], in0=nmr3_b[:],
                        scalar=rsumT_g[:, ec:ec + 1],
                        in1=gt_t[ec % 2][:], op0=ALU.mult, op1=ALU.add)
                    if ec >= 1:
                        vector.wait_ge(m_silu.h, SI + ec)
                        vector.drain()
                        ins = vector.scalar_tensor_tensor(
                            out=hid[:, (ec - 1) * CQ:ec * CQ],
                            in0=gt_t[(ec - 1) % 2][:], scalar=sc_b[:, 1:2],
                            in1=silu_f[(ec - 1) % 2][:],
                            op0=ALU.mult, op1=ALU.mult)
                        m_hid.inc(ins, 1)
                vector.wait_ge(m_silu.h, SI + ECH)
                vector.drain()
                ins = vector.scalar_tensor_tensor(
                    out=hid[:, (ECH - 1) * CQ:ECH * CQ],
                    in0=gt_t[(ECH - 1) % 2][:], scalar=sc_b[:, 1:2],
                    in1=silu_f[(ECH - 1) % 2][:], op0=ALU.mult, op1=ALU.mult)
                m_hid.inc(ins, 1)

            @block.scalar
            def _(scalar):
                for ec in range(ECH):
                    scalar.wait_ge(m_ev.h, EV + 3 * ec + 2)
                    reuse_wait(scalar, m_hid, HD, ec)
                    ins = scalar.activation(silu_f[ec % 2][:], au_t[ec % 2][:],
                                            AF.Silu, scale=sc_b[:, 0:1])
                    m_silu.inc(ins, 1)

            b_wd = d_wd.base()
            b_wd2 = d_wd2.base()
            b_x3l = d_x3l.base()
            b_y4st = d_y4st.base()
            DSL, DE = m_dnsl.n, m_dnev.n
            Y4C, Y4S, ST4 = m_y4c.n, m_y4s.n, m_st4.n
            XC, XS3, ST3 = m_x3c.n, m_x3s.n, m_st3.n
            V4B = m_var.n

            nwd = NDG * NEB

            def wd_load(eng, stream, i):
                reuse_wait(eng, m_dnsl, DSL, i, nbuf=4)
                eb = i % NEB
                dg = i // NEB
                e0 = eb * EBC * 128
                d0 = dg * DG * 128
                ins = eng.dma_start(
                    out=wd_sl[i % 4][:].rearrange(
                        "p (es d) -> p es d", es=EBC),
                    in_=tern_d[e0:e0 + EBC * 128,
                               d0:d0 + DG * 128].rearrange(
                        "(es p) d -> p es d", p=128))
                stream.inc(ins, i // 2)

            @block.sync
            def _(sync):
                if qq == 0:
                    d_x2st.wait_all(sync)
                    sync.wait_ge(cc2.h, 4)
                for i in range(0, nwd, 2):
                    wd_load(sync, d_wd, i)

            if not last:
                b_wug_next = d_wug.base()

                @block.gpsimd
                def _(gpsimd):
                    gpsimd.wait_ge(m_gtmm.h, m_gtmm.n)
                    for eg in range(min(2, NEG)):
                        wug_load(gpsimd, eg)
                    for cc in range(CC):
                        if cc >= 2:
                            gpsimd.wait_ge(m_x3c.h, XC + cc - 1)
                            gpsimd.wait_ge(m_x3s.h, XS3 + cc - 1)
                        elif XC > 0:
                            gpsimd.wait_ge(m_x3c.h, XC)
                            gpsimd.wait_ge(m_x3s.h, XS3)
                        ins = gpsimd.dma_start(
                            out=x2f[cc % 2][:],
                            in_=x2_sp[cc * 128:(cc + 1) * 128,
                                      q0 + CQ:q0 + 2 * CQ])
                        d_x3l.inc(ins, cc)

            def c_stats(tensor, dc):
                tensor.wait_ge(m_y4c.h, Y4C + dc + 1)
                tensor.wait_ge(m_y4s.h, Y4S + dc + 1)
                for (off, w) in _slices(CQ):
                    tensor.matmul(ps_sum4[0:1, off:off + w], ones_col_bf[:],
                                  y4bf[dc % 2][:, off:off + w],
                                  start=(dc == 0), stop=(dc == CC - 1))
                    ins = tensor.matmul(ps_sq4[0:1, off:off + w],
                                        ones_col_bf[:],
                                        y4sq[dc % 2][:, off:off + w],
                                        start=(dc == 0), stop=(dc == CC - 1))
                m_st4.inc(ins, 1)

            def c_stats3(tensor, cc):
                if cc == 0:
                    tensor.wait_ge(m_ev.h, m_ev.n)
                tensor.wait_ge(m_x3c.h, XC + cc + 1)
                tensor.wait_ge(m_x3s.h, XS3 + cc + 1)
                for (off, w) in _slices(CQ):
                    tensor.matmul(ps_up[0][0:1, off:off + w], ones_col_bf[:],
                                  x2q_bf[(qq + 1) % 2][:, cc * CQ + off:
                                                       cc * CQ + off + w],
                                  start=(cc == 0), stop=(cc == CC - 1))
                    ins = tensor.matmul(ps_gt[0][0:1, off:off + w],
                                        ones_col_bf[:],
                                        x2sq[cc % 2][:, off:off + w],
                                        start=(cc == 0), stop=(cc == CC - 1))
                m_st3.inc(ins, 1)

            @block.tensor
            def _(tensor):
                tensor.wait_ge(m_hid.h, HD + ECH)
                tensor.wait_ge(m_bc.h, bc4tar_prev)
                if qq == 0:
                    tensor.wait_ge(m_sclev.h, RSDONE)
                if qq >= 1:
                    tensor.wait_ge(m_var.h, V4B)
                for dg in range(NDG):
                    if dg >= 1:
                        tensor.wait_ge(m_dnev.h, DE + dg * DG)
                    for eb in range(NEB):
                        i = dg * NEB + eb
                        if i % 2 == 0:
                            d_wd.wait_item(tensor, b_wd, i // 2)
                        else:
                            d_wd2.wait_item(tensor, b_wd2, i // 2)
                        for dl in range(DG):
                            for el in range(EBC):
                                ec = eb * EBC + el
                                for (off, w) in _slices(CQ):
                                    ins = tensor.matmul(
                                        ps_dn[dl][:, off:off + w],
                                        wd_sl[i % 4][:, el * DG * 128 + dl * 128:
                                                      el * DG * 128 + (dl + 1) * 128],
                                        hid[:, ec * CQ + off:ec * CQ + off + w],
                                        start=(eb == 0 and el == 0),
                                        stop=(eb == NEB - 1 and el == EBC - 1))
                        m_dnsl.inc(ins, 1)
                    if dg >= 1:
                        for dl in range(DG):
                            c_stats(tensor, (dg - 1) * DG + dl)
                    if not last:
                        nd2 = max(1, NDG // 2)
                        cpd2 = CC // nd2
                        if dg < nd2:
                            for k_ in range(cpd2):
                                c_stats3(tensor, dg * cpd2 + k_)
                for dl in range(DG):
                    c_stats(tensor, (NDG - 1) * DG + dl)

            @block.scalar
            def _(scalar):
                if qq >= 1:
                    scalar.wait_ge(m_y4c.h, Y4C)
                nsq = [0]
                for dg in range(NDG):
                    for i in range(dg * NEB + 1, (dg + 1) * NEB, 2):
                        wd_load(scalar, d_wd2, i)
                    for dl in range(DG):
                        dc = dg * DG + dl
                        if not last:
                            while nsq[0] < CC and nsq[0] <= (dc + 1) * 2 - 1:
                                cc = nsq[0]
                                d_x3l.wait_item(scalar, b_x3l, cc)
                                reuse_wait(scalar, m_st3, ST3, cc)
                                ins = scalar.activation(x2sq[cc % 2][:],
                                                        x2f[cc % 2][:],
                                                        AF.Square)
                                m_x3s.inc(ins, 1)
                                nsq[0] += 1
                        scalar.wait_ge(m_dnsl.h, DSL + (dg + 1) * NEB)
                        d_y4st.wait_prev(scalar, b_y4st, dc)
                        reuse_wait(scalar, m_y4c, Y4C, dc)
                        reuse_wait(scalar, m_st4, ST4, dc)
                        ins = scalar.mul(y4_f[dc % 2][:], ps_dn[dl][:, 0:CQ],
                                         sc_b[:, 2:3])
                        m_dnev.inc(ins, 1)
                        scalar.drain()
                        ins = scalar.activation(y4sq[dc % 2][:], y4_f[dc % 2][:],
                                                AF.Square)
                        m_y4s.inc(ins, 1)
                        ins = scalar.dma_start(
                            out=y4_sp[dc * 128:(dc + 1) * 128, q0:q0 + CQ],
                            in_=y4_f[dc % 2][:])
                        d_y4st.inc(ins, dc)

            @block.vector
            def _(vector):
                ncst = [0]
                for dc in range(CC):
                    if not last:
                        while ncst[0] < CC and ncst[0] <= (dc + 1) * 2 - 1:
                            cc = ncst[0]
                            d_x3l.wait_item(vector, b_x3l, cc)
                            ins = vector.tensor_copy(
                                x2q_bf[(qq + 1) % 2][:, cc * CQ:(cc + 1) * CQ],
                                x2f[cc % 2][:])
                            m_x3c.inc(ins, 1)
                            ncst[0] += 1
                    vector.wait_ge(m_y4s.h, Y4S + dc + 1)
                    reuse_wait(vector, m_st4, ST4, dc)
                    ins = vector.tensor_copy(y4bf[dc % 2][:], y4_f[dc % 2][:])
                    m_y4c.inc(ins, 1)

            if not last:
                r3tar = emit_rows(CQ, ps_up[0], ps_gt[0], m_st3, ST3 + CC,
                                  sum3, sq3, r3_row, nmr3_row)
                v3guard = m_var.n
            r4tar = emit_rows(CQ, ps_sum4, ps_sq4, m_st4, ST4 + CC,
                              sum4, sq4, r4_row, nmr4_row)
            if not last:
                bc3tar = emit_bcast(CQ, r3_row, nmr3_row, r3_b, nmr3_b, r3tar,
                                    ps_up[1], ps_gt[1],
                                    extra_pe_waits=((m_ev, m_ev.n),))
            bc4tar = emit_bcast(CQ, r4_row, nmr4_row, r4_b, nmr4_b, r4tar,
                                ps_dn[0], ps_dn[1],
                                extra_pe_waits=((m_dnev, DE + NDG * DG),
                                                (m_on, m_on.n),
                                                (m_onp, m_onp.n)))
            bc4tar_prev = bc4tar

            b_xfl = d_xfl.base()
            b_yfl = d_yfl.base()
            b_out = d_out.base()
            ON, ONP = m_on.n, m_onp.n

            @block.sync
            def _(sync):
                for cc in range(CC):
                    if cc >= 3:
                        j = cc - 3
                        if j % 2 == 0:
                            sync.wait_ge(m_on.h, ON + j // 2 + 1)
                        else:
                            sync.wait_ge(m_onp.h, ONP + (j + 1) // 2)
                    else:
                        if ON > 0:
                            sync.wait_ge(m_on.h, ON)
                        if ONP > 0:
                            sync.wait_ge(m_onp.h, ONP)
                    if cc >= 2:
                        d_xfl.wait_item(sync, b_xfl, cc - 2)
                        d_yfl.wait_item(sync, b_yfl, cc - 2)
                    ins = sync.dma_start(
                        out=x2st_[cc % 3][:],
                        in_=x2_sp[cc * 128:(cc + 1) * 128, q0:q0 + CQ])
                    d_xfl.inc(ins, cc)
                    d_y4st.wait_item(sync, b_y4st, cc)
                    ins = sync.dma_start(
                        out=y4res[cc % 3][:],
                        in_=y4_sp[cc * 128:(cc + 1) * 128, q0:q0 + CQ])
                    d_yfl.inc(ins, cc)

            @block.vector
            def _(vector):
                vector.wait_ge(m_bc.h, bc4tar)
                for cc in range(0, CC, 2):
                    d_xfl.wait_item(vector, b_xfl, cc)
                    d_yfl.wait_item(vector, b_yfl, cc)
                    d_out.wait_prev(vector, b_out, cc)
                    vector.tensor_tensor(out=outt[0][:], in0=y4res[cc % 3][:],
                                         in1=r4_b[:], op=ALU.mult)
                    vector.drain()
                    vector.tensor_tensor(out=outt[0][:], in0=outt[0][:],
                                         in1=nmr4_b[:], op=ALU.add)
                    vector.drain()
                    ins = vector.tensor_tensor(out=outt[0][:], in0=outt[0][:],
                                               in1=x2st_[cc % 3][:], op=ALU.add)
                    m_on.inc(ins, 1)

            @block.gpsimd
            def _(gpsimd):
                gpsimd.wait_ge(m_bc.h, bc4tar)
                for cc in range(1, CC, 2):
                    d_xfl.wait_item(gpsimd, b_xfl, cc)
                    d_yfl.wait_item(gpsimd, b_yfl, cc)
                    d_out.wait_prev(gpsimd, b_out, cc)
                    gpsimd.tensor_tensor(out=outt[1][:], in0=y4res[cc % 3][:],
                                         in1=r4_b[:], op=ALU.mult)
                    gpsimd.drain()
                    gpsimd.tensor_tensor(out=outt[1][:], in0=outt[1][:],
                                         in1=nmr4_b[:], op=ALU.add)
                    gpsimd.drain()
                    ins = gpsimd.tensor_tensor(out=outt[1][:], in0=outt[1][:],
                                               in1=x2st_[cc % 3][:],
                                               op=ALU.add)
                    m_onp.inc(ins, 1)

            @block.scalar
            def _(scalar):
                for cc in range(CC):
                    if cc % 2 == 0:
                        scalar.wait_ge(m_on.h, ON + cc // 2 + 1)
                    else:
                        scalar.wait_ge(m_onp.h, ONP + (cc + 1) // 2)
                    ins = scalar.dma_start(
                        out=out_fm[cc * 128:(cc + 1) * 128, q0:q0 + CQ],
                        in_=outt[cc % 2][:])
                    d_out.inc(ins, cc)

            if not last:
                b_wug = b_wug_next

        c.close()

        @block.sync
        def _(sync):
            d_out.wait_all(sync)

    return nc


def _make_consts():
    mask = np.zeros((128, 129), np.float32)
    for j in range(4):
        mask[32 * j:32 * (j + 1), 32 * j:32 * (j + 1)] = 1.0
    mask[:, 128] = 1.0
    p4 = np.zeros((4, 128), np.float32)
    for j in range(4):
        p4[j, 32 * j:32 * (j + 1)] = 1.0
    bm4 = np.zeros((128, 4), np.float32)
    for j in range(4):
        bm4[32 * j:32 * (j + 1), j] = 1.0
    return mask, p4, bm4


def make_in_maps(p, x, w_qkv, w_proj, b_proj, W_up, W_gate, W_down):
    xT = np.ascontiguousarray(np.transpose(x, (0, 2, 1)))
    wqkvT = np.ascontiguousarray(w_qkv.T)
    wprojT = np.ascontiguousarray(w_proj.T)
    bp_ = np.ascontiguousarray(b_proj.reshape(1, -1))
    wupT = np.ascontiguousarray(W_up.T)
    wgateT = np.ascontiguousarray(W_gate.T)
    wdownT = np.ascontiguousarray(W_down.T)
    mask, p4, bm4 = _make_consts()
    in_maps = []
    for k in range(p.n_cores):
        b_ = k // p.cores_per_b
        t0 = (k % p.cores_per_b) * p.T
        in_maps.append({
            "x_fm": np.ascontiguousarray(xT[b_, :, t0:t0 + p.T]),
            "wqkvT": wqkvT,
            "wprojT": wprojT,
            "bproj": bp_,
            "wup_sl": np.ascontiguousarray(wupT[k * p.RU:(k + 1) * p.RU]),
            "wgate_sl": np.ascontiguousarray(wgateT[k * p.RU:(k + 1) * p.RU]),
            "wdown_sl": np.ascontiguousarray(wdownT[k * p.RD:(k + 1) * p.RD]),
            "mask129": mask,
            "id128": np.eye(128, dtype=np.float32),
            "p4m": p4,
            "bm4": bm4,
        })
    return in_maps


def assemble_out(p, results):
    outT = np.empty((2, p.C, p.cores_per_b * p.T), np.float32)
    for k in range(p.n_cores):
        b_ = k // p.cores_per_b
        t0 = (k % p.cores_per_b) * p.T
        outT[b_, :, t0:t0 + p.T] = results[k]["out_fm"]
    return np.ascontiguousarray(np.transpose(outT, (0, 2, 1)))


def kernel(L0_attn_x, L0_attn_w_qkv, L0_attn_w_proj, L0_attn_b_proj,
           L0_ffn_W_up, L0_ffn_W_gate, L0_ffn_W_down):
    p = Params()
    nc = build_nc(p)
    in_maps = make_in_maps(p, L0_attn_x, L0_attn_w_qkv, L0_attn_w_proj,
                           L0_attn_b_proj, L0_ffn_W_up, L0_ffn_W_gate,
                           L0_ffn_W_down)
    res = run_bass_kernel_spmd(nc, in_maps, core_ids=list(range(p.n_cores)))
    return assemble_out(p, res.results)


# version 19
# speedup vs baseline: 1.0177x; 1.0177x over previous
from contextlib import ExitStack

import numpy as np
import concourse.bass as bass
import concourse.mybir as mybir
from concourse.bass_utils import run_bass_kernel_spmd

F32 = mybir.dt.float32
BF16 = mybir.dt.bfloat16
AF = mybir.ActivationFunctionType
ALU = mybir.AluOpType
AX = mybir.AxisListType.X

LN_EPS = 1e-5
ATTN_EPS = 1e-15
QUANT_EPS = 1e-8


def _slices(span, width=512):
    out, off = [], 0
    while off < span:
        w = min(width, span - off)
        out.append((off, w))
        off += w
    return out


class Params:
    def __init__(self, C=2048, T=2048, E=8192, n_cores=8):
        self.C, self.T, self.E, self.n_cores = C, T, E, n_cores
        self.cores_per_b = n_cores // 2
        self.CC = C // 128
        self.TC = T // 128
        self.ECH = E // 128
        self.G = C // 128
        self.HALF = T // 2
        self.CQ = max(T // 4, 128)
        self.OGW = min(512, C)
        self.NOG = C // self.OGW
        self.QG = self.OGW // 128
        self.RU = C // n_cores
        self.RD = E // n_cores
        self.QW = min(2048, max(E, C))
        self.EG = 2
        self.DG = 2
        self.EB = min(4, self.ECH)
        self.NRU = (self.RU + 127) // 128


class Sem:
    __slots__ = ("h", "n")

    def __init__(self, ctx, nc, name):
        self.h = ctx.enter_context(nc.semaphore(name))
        self.n = 0

    def inc(self, inst, k=1):
        inst.then_inc(self.h, k)
        self.n += k
        return self.n


class DS:
    __slots__ = ("s", "unit")

    def __init__(self, ctx, nc, name, unit=16):
        self.s = [Sem(ctx, nc, name + "a"), Sem(ctx, nc, name + "b")]
        self.unit = unit

    def base(self):
        return (self.s[0].n, self.s[1].n)

    def inc(self, inst, i, k=16):
        sem_ = self.s[i % 2]
        inst.then_inc(sem_.h, k)
        sem_.n += k

    def wait_item(self, eng, b, i):
        eng.wait_ge(self.s[i % 2].h, b[i % 2] + self.unit * (i // 2 + 1))

    def wait_prev(self, eng, b, i):
        if i >= 2:
            self.wait_item(eng, b, i - 2)
        elif b[i % 2] > 0:
            eng.wait_ge(self.s[i % 2].h, b[i % 2])

    def wait_all(self, eng):
        eng.wait_ge(self.s[0].h, self.s[0].n)
        eng.wait_ge(self.s[1].h, self.s[1].n)


def build_nc(p: Params):
    nc = bass.Bass("TRN2", target_bir_lowering=False)
    C, T, E = p.C, p.T, p.E
    CC, TC, ECH, G, HALF, CQ = p.CC, p.TC, p.ECH, p.G, p.HALF, p.CQ
    NQ = T // CQ

    x_in = nc.dram_tensor("x_fm", [C, T], F32, kind="ExternalInput").ap()
    wqkvT = nc.dram_tensor("wqkvT", [C, 3 * C], F32, kind="ExternalInput").ap()
    wprojT = nc.dram_tensor("wprojT", [C, C], F32, kind="ExternalInput").ap()
    bproj = nc.dram_tensor("bproj", [1, C], F32, kind="ExternalInput").ap()
    wup_sl = nc.dram_tensor("wup_sl", [p.RU, E], F32, kind="ExternalInput").ap()
    wgate_sl = nc.dram_tensor("wgate_sl", [p.RU, E], F32, kind="ExternalInput").ap()
    wdown_sl = nc.dram_tensor("wdown_sl", [p.RD, C], F32, kind="ExternalInput").ap()
    mask129 = nc.dram_tensor("mask129", [128, 129], F32, kind="ExternalInput").ap()
    id128 = nc.dram_tensor("id128", [128, 128], F32, kind="ExternalInput").ap()
    p4m = nc.dram_tensor("p4m", [4, 128], F32, kind="ExternalInput").ap()
    bm4 = nc.dram_tensor("bm4", [128, 4], F32, kind="ExternalInput").ap()

    out_fm = nc.dram_tensor("out_fm", [C, T], F32, kind="ExternalOutput").ap()

    q_sp = nc.dram_tensor("q_sp", [G * 128, T], BF16).ap()
    x2_sp = nc.dram_tensor("x2_sp", [C, T], F32).ap()
    y4_sp = nc.dram_tensor("y4_sp", [C, T], F32).ap()
    scl_in = nc.dram_tensor("scl_in", [1, 3], F32).ap()
    scl_out = nc.dram_tensor("scl_out", [1, 3], F32).ap()
    scl_bc = nc.dram_tensor("scl_bc", [1, 12], F32).ap()
    tern_u_sl = nc.dram_tensor("tern_u_sl", [p.RU, E], BF16).ap()
    tern_g_sl = nc.dram_tensor("tern_g_sl", [p.RU, E], BF16).ap()
    tern_d_sl = nc.dram_tensor("tern_d_sl", [p.RD, C], BF16).ap()
    tern_u = nc.dram_tensor("tern_u", [C, E], BF16).ap()
    tern_g = nc.dram_tensor("tern_g", [C, E], BF16).ap()
    tern_d = nc.dram_tensor("tern_d", [E, C], BF16).ap()
    vk_part = nc.dram_tensor("vk_part", [G * 128, 129], F32).ap()
    vk_red = nc.dram_tensor("vk_red", [G * 128, 129], F32).ap()
    NRW = 2 * p.NRU
    rsum_part = nc.dram_tensor("rsum_part", [NRW, E], F32).ap()
    rsum_red = nc.dram_tensor("rsum_red", [NRW, E], F32).ap()

    all_cores = [list(range(p.n_cores))]
    batch_groups = [list(range(p.cores_per_b)),
                    list(range(p.cores_per_b, p.n_cores))]

    tiles = []
    for widx, (src, dst, R, W_) in enumerate(
            ((wup_sl, tern_u_sl, p.RU, E),
             (wgate_sl, tern_g_sl, p.RU, E),
             (wdown_sl, tern_d_sl, p.RD, C))):
        r = 0
        while r < R:
            rows = min(128, R - r)
            for (c0, w) in _slices(W_, p.QW):
                tiles.append((src, dst, r, rows, c0, w, widx))
            r += rows
    NTL = len(tiles)

    with ExitStack() as top:
        _uid = [0]

        def sb(ctx, name, shape, dt):
            _uid[0] += 1
            return ctx.enter_context(
                nc.sbuf_tensor(f"{name}_{_uid[0]}", shape, dt))

        def psum(ctx, name, shape, dt=F32):
            _uid[0] += 1
            return ctx.enter_context(
                nc.psum_tensor(f"{name}_{_uid[0]}", shape, dt))

        def sem(name):
            return Sem(top, nc, name)

        def ds(name, unit=16):
            return DS(top, nc, name, unit)

        def reuse_wait(eng, s, base, i, unit=1, nbuf=2):
            if i >= nbuf:
                eng.wait_ge(s.h, base + unit * (i - nbuf + 1))
            elif base > 0:
                eng.wait_ge(s.h, base)

        ones_col_bf = sb(top, "ones_col_bf", [128, 1], BF16)
        ones_col_f = sb(top, "ones_col_f", [128, 1], F32)
        ones_row_f = sb(top, "ones_row_f", [1, 128], F32)
        ones_row_bf = sb(top, "ones_row_bf", [1, 512], BF16)
        mask129_s = sb(top, "mask129_s", [128, 129], F32)
        p4_bf = sb(top, "p4_bf", [4, 128], BF16)
        bm4_s = sb(top, "bm4_s", [128, 4], F32)
        sc_b = sb(top, "sc_b", [128, 12], F32)

        block = top.enter_context(nc.Block())

        d_sl1 = ds("d_sl1"); d_sl2 = ds("d_sl2"); d_tern = ds("d_tern")
        d_x1 = ds("d_x1"); d_x2l = ds("d_x2l")
        d_wq = ds("d_wq"); d_qsp = ds("d_qsp")
        d_wkv = ds("d_wkv", unit=32)
        d_qld = ds("d_qld"); d_wp = ds("d_wp"); d_xrl = ds("d_xrl")
        d_x2st = ds("d_x2st")
        d_x3l = ds("d_x3l"); d_rs = ds("d_rs"); d_wd2 = ds("d_wd2")
        d_wug = ds("d_wug", unit=32); d_wd = ds("d_wd"); d_y4st = ds("d_y4st")
        d_xfl = ds("d_xfl"); d_yfl = ds("d_yfl"); d_out = ds("d_out")

        m_const = sem("m_const"); m_gpc = sem("m_gpc"); m_gpio = sem("m_gpio")
        m_nst = sem("m_nst"); m_vkst = sem("m_vkst")
        cc1 = sem("cc1"); cc2 = sem("cc2"); cc3 = sem("cc3")
        m_sum = sem("m_sum"); m_sclmm = sem("m_sclmm"); m_sclev = sem("m_sclev")
        m_scrow = sem("m_scrow"); m_q = sem("m_q")
        m_xc = sem("m_xc"); m_xs = sem("m_xs"); m_st1 = sem("m_st1")
        m_n1 = sem("m_n1")
        m_qmm = sem("m_qmm"); m_qev = sem("m_qev")
        m_kvmm = sem("m_kvmm"); m_kev = sem("m_kev"); m_vev = sem("m_vev")
        m_vkmm = sem("m_vkmm"); m_vkms = sem("m_vkms"); m_vone = sem("m_vone")
        m_den = sem("m_den")
        m_bnum = sem("m_bnum"); m_bts = sem("m_bts"); m_brec = sem("m_brec")
        m_bdiv = sem("m_bdiv"); m_bnev = sem("m_bnev"); m_batn = sem("m_batn")
        m_pmm = sem("m_pmm"); m_pev = sem("m_pev")
        m_ybf = sem("m_ybf"); m_ysq = sem("m_ysq"); m_st2 = sem("m_st2")
        m_x2n = sem("m_x2n")
        m_x3c = sem("m_x3c"); m_x3s = sem("m_x3s"); m_st3 = sem("m_st3")
        m_upmm = sem("m_upmm"); m_gtmm = sem("m_gtmm"); m_silu = sem("m_silu")
        m_hid = sem("m_hid")
        m_ev = sem("m_ev")
        m_dnsl = sem("m_dnsl"); m_dnev = sem("m_dnev")
        m_y4c = sem("m_y4c"); m_y4s = sem("m_y4s"); m_st4 = sem("m_st4")
        m_on = sem("m_on"); m_onp = sem("m_onp")
        m_var = sem("m_var"); m_sqr = sem("m_sqr"); m_row = sem("m_row")
        m_bcpe = sem("m_bcpe"); m_bc = sem("m_bc")

        def emit_rows(Tspan, ps_sum, ps_sq, m_pe_stats, pe_target,
                      s_row, q_row, rr, nmrr, scratch=None):
            msq = nmrr if scratch is None else scratch

            @block.vector
            def _(vector):
                vector.wait_ge(m_pe_stats.h, pe_target)
                vector.tensor_copy(s_row[0:1, 0:Tspan], ps_sum[0:1, 0:Tspan])
                vector.tensor_copy(q_row[0:1, 0:Tspan], ps_sq[0:1, 0:Tspan])
                vector.drain()
                vector.tensor_scalar(out=s_row[0:1, 0:Tspan],
                                     in0=s_row[0:1, 0:Tspan],
                                     scalar1=1.0 / C, scalar2=None, op0=ALU.mult)
                vector.drain()
                vector.tensor_tensor(out=msq[0:1, 0:Tspan],
                                     in0=s_row[0:1, 0:Tspan],
                                     in1=s_row[0:1, 0:Tspan], op=ALU.mult)
                vector.drain()
                vector.scalar_tensor_tensor(
                    out=q_row[0:1, 0:Tspan], in0=q_row[0:1, 0:Tspan],
                    scalar=1.0 / C, in1=msq[0:1, 0:Tspan],
                    op0=ALU.mult, op1=ALU.subtract)
                vector.drain()
                ins = vector.tensor_scalar(out=q_row[0:1, 0:Tspan],
                                           in0=q_row[0:1, 0:Tspan],
                                           scalar1=LN_EPS, scalar2=None,
                                           op0=ALU.add)
                m_var.inc(ins, 1)
            vtar = m_var.n

            @block.scalar
            def _(scalar):
                scalar.wait_ge(m_var.h, vtar)
                ins = scalar.activation(q_row[0:1, 0:Tspan], q_row[0:1, 0:Tspan],
                                        AF.Sqrt)
                m_sqr.inc(ins, 1)
            star = m_sqr.n

            @block.vector
            def _(vector):
                vector.wait_ge(m_sqr.h, star)
                vector.reciprocal(out=rr[0:1, 0:Tspan], in_=q_row[0:1, 0:Tspan])
                vector.drain()
                ins = vector.scalar_tensor_tensor(
                    out=nmrr[0:1, 0:Tspan], in0=s_row[0:1, 0:Tspan], scalar=-1.0,
                    in1=rr[0:1, 0:Tspan], op0=ALU.mult, op1=ALU.mult)
                m_row.inc(ins, 1)
            return m_row.n

        def emit_bcast(Tspan, rr, nmrr, rbt, nmrbt, row_target,
                       ps0, ps1, extra_pe_waits=()):
            sls = _slices(Tspan)
            pe_base, ev_base = m_bcpe.n, m_bc.n

            @block.tensor
            def _(tensor):
                tensor.wait_ge(m_row.h, row_target)
                for (s_, v_) in extra_pe_waits:
                    tensor.wait_ge(s_.h, v_)
                for si, (off, w) in enumerate(sls):
                    if si >= 1:
                        tensor.wait_ge(m_bc.h, ev_base + si)
                    elif ev_base > 0:
                        tensor.wait_ge(m_bc.h, ev_base)
                    tensor.matmul(ps0[:, 0:w], ones_row_f[:],
                                  rr[0:1, off:off + w], start=True, stop=True)
                    ins = tensor.matmul(ps1[:, 0:w], ones_row_f[:],
                                        nmrr[0:1, off:off + w],
                                        start=True, stop=True)
                    m_bcpe.inc(ins, 1)

            @block.vector
            def _(vector):
                for si, (off, w) in enumerate(sls):
                    vector.wait_ge(m_bcpe.h, pe_base + si + 1)
                    vector.tensor_copy(rbt[:, off:off + w], ps0[:, 0:w])
                    ins = vector.tensor_copy(nmrbt[:, off:off + w], ps1[:, 0:w])
                    m_bc.inc(ins, 1)
            return m_bc.n

        xln_ctx = ExitStack()
        x_ln1 = sb(xln_ctx, "x_ln1", [128, CC * T], BF16)
        vk_sb = sb(xln_ctx, "vk_sb", [128, G * 129], F32)
        a3w = ExitStack()
        OGW, NOG, QG = p.OGW, p.NOG, p.QG
        wk_sl = [sb(a3w, "wk_sl0", [128, CC * OGW], BF16), None]
        wv_sl = [sb(a3w, "wv_sl0", [128, CC * OGW], BF16), None]

        def kv_load(gpsimd, og):
            ins = gpsimd.dma_start(
                out=wk_sl[og % 2][:].rearrange("p (cc o) -> p cc o", cc=CC),
                in_=wqkvT[:, C + og * OGW:C + (og + 1) * OGW].rearrange(
                    "(cc p) o -> p cc o", p=128))
            d_wkv.inc(ins, og)
            ins = gpsimd.dma_start(
                out=wv_sl[og % 2][:].rearrange("p (cc o) -> p cc o", cc=CC),
                in_=wqkvT[:, 2 * C + og * OGW:2 * C + (og + 1) * OGW].rearrange(
                    "(cc p) o -> p cc o", p=128))
            d_wkv.inc(ins, og)

        sctx = ExitStack()
        qst = [sb(sctx, f"qst{i}", [128, p.QW], F32) for i in range(2)]
        tst = [sb(sctx, f"tst{i}", [128, p.QW], BF16) for i in range(4)]
        rs_row = sb(sctx, "rs_row", [1, min(512, p.QW)], F32)
        sums_s = sb(sctx, "sums_s", [128, NTL], F32)
        scl_row = sb(sctx, "scl_row", [1, 3], F32)
        sc_row12 = sb(sctx, "sc_row12", [1, 12], F32)

        @block.sync
        def _(sync):
            for ap_, dst in ((mask129, mask129_s), (bm4, bm4_s)):
                sync.dma_start(out=dst[:], in_=ap_[:, :]).then_inc(m_const.h, 16)
                m_const.n += 16
            for i, (src, dst, r0, rows, c0, w, widx) in enumerate(tiles):
                reuse_wait(sync, m_sum, 0, i)
                ins = sync.dma_start(out=qst[i % 2][0:rows, 0:w],
                                     in_=src[r0:r0 + rows, c0:c0 + w])
                d_sl1.inc(ins, i)

        @block.gpsimd
        def _(gpsimd):
            gpsimd.dma_start(out=p4_bf[:], in_=p4m[:, :]).then_inc(m_gpc.h, 16)
            m_gpc.n += 16

        b_sl1 = ((0, 0))

        @block.vector
        def _(vector):
            vector.memset(ones_col_bf[:], 1.0)
            vector.memset(ones_col_f[:], 1.0)
            vector.memset(ones_row_f[:], 1.0)
            vector.memset(ones_row_bf[:], 1.0)
            for i, (src, dst, r0, rows, c0, w, widx) in enumerate(tiles):
                d_sl1.wait_item(vector, (0, 0), i)
                ins = vector.reduce_sum(out=sums_s[0:rows, i:i + 1],
                                        in_=qst[i % 2][0:rows, 0:w], axis=AX,
                                        apply_absolute_value=True)
                m_sum.inc(ins, 1)
        assert m_sum.n == NTL

        sclp = ExitStack()
        ps_scl = psum(sclp, "ps_scl", [1, 3])

        @block.tensor
        def _(tensor):
            tensor.wait_ge(m_sum.h, NTL)
            for widx in range(3):
                sel = [i for i, t_ in enumerate(tiles) if t_[6] == widx]
                for j, i in enumerate(sel):
                    rows = tiles[i][3]
                    ins = tensor.matmul(ps_scl[0:1, widx:widx + 1],
                                        sums_s[0:rows, i:i + 1],
                                        ones_col_f[0:rows, :],
                                        start=(j == 0), stop=(j == len(sel) - 1))
            m_sclmm.inc(ins, 1)

        @block.vector
        def _(vector):
            vector.wait_ge(m_sclmm.h, 1)
            ins = vector.tensor_copy(scl_row[0:1, 0:3], ps_scl[0:1, 0:3])
            m_sclev.inc(ins, 1)

        sclp.close()

        a1 = ExitStack()
        xst = [sb(a1, f"xst{i}", [128, T], F32) for i in range(2)]
        xbf = [sb(a1, f"xbf{i}", [128, T], BF16) for i in range(2)]
        xsq = [sb(a1, f"xsq{i}", [128, T], BF16) for i in range(2)]
        sum_row = sb(a1, "sum_row", [1, T], F32)
        sq_row = sb(a1, "sq_row", [1, T], F32)
        r_b = sb(a1, "r_b", [128, T], F32)
        nmr_b = sb(a1, "nmr_b", [128, T], F32)

        @block.sync
        def _(sync):
            for cc in range(CC):
                if cc >= 2:
                    sync.wait_ge(m_xc.h, cc - 1)
                    sync.wait_ge(m_xs.h, cc - 1)
                ins = sync.dma_start(out=xst[cc % 2][:],
                                     in_=x_in[cc * 128:(cc + 1) * 128, :])
                d_x1.inc(ins, cc)

        @block.vector
        def _(vector):
            for cc in range(CC):
                d_x1.wait_item(vector, (0, 0), cc)
                if cc >= 2:
                    vector.wait_ge(m_st1.h, cc - 1)
                ins = vector.tensor_copy(xbf[cc % 2][:], xst[cc % 2][:])
                m_xc.inc(ins, 1)

        @block.scalar
        def _(scalar):
            for cc in range(CC):
                d_x1.wait_item(scalar, (0, 0), cc)
                if cc >= 2:
                    scalar.wait_ge(m_st1.h, cc - 1)
                ins = scalar.activation(xsq[cc % 2][:], xst[cc % 2][:], AF.Square)
                m_xs.inc(ins, 1)

        a1p = ExitStack()
        ps_sum1 = psum(a1p, "ps_sum1", [1, T])
        ps_sq1 = psum(a1p, "ps_sq1", [1, T])

        @block.tensor
        def _(tensor):
            tensor.wait_ge(m_sclev.h, 1)
            for cc in range(CC):
                tensor.wait_ge(m_xc.h, cc + 1)
                tensor.wait_ge(m_xs.h, cc + 1)
                for (off, w) in _slices(T):
                    tensor.matmul(ps_sum1[0:1, off:off + w], ones_col_bf[:],
                                  xbf[cc % 2][:, off:off + w],
                                  start=(cc == 0), stop=(cc == CC - 1))
                    ins = tensor.matmul(ps_sq1[0:1, off:off + w], ones_col_bf[:],
                                        xsq[cc % 2][:, off:off + w],
                                        start=(cc == 0), stop=(cc == CC - 1))
                m_st1.inc(ins, 1)
        assert m_st1.n == CC

        r1tar = emit_rows(T, ps_sum1, ps_sq1, m_st1, CC, sum_row, sq_row,
                          sq_row, sum_row, scratch=r_b[0:1, :])

        @block.gpsimd
        def _(gpsimd):
            gpsimd.wait_ge(m_sclev.h, 1)
            gpsimd.dma_start(out=scl_in, in_=scl_row[0:1, :]).then_inc(m_gpio.h, 16)
            m_gpio.n += 16
            gpsimd.wait_ge(m_gpio.h, 16)
            ins = gpsimd.collective_compute(
                "AllReduce", ALU.add, replica_groups=all_cores,
                ins=[scl_in.opt()], outs=[scl_out.opt()])
            cc1.inc(ins, 1)
            gpsimd.wait_ge(cc1.h, 1)
            gpsimd.dma_start(out=scl_row[0:1, :], in_=scl_out).then_inc(m_gpio.h, 16)
            m_gpio.n += 16

        @block.vector
        def _(vector):
            vector.wait_ge(m_gpio.h, 32)
            inv_n = 1.0 / float(C * E)
            vector.memset(sc_row12[:], 0.0)
            vector.drain()
            vector.tensor_scalar(out=sc_row12[0:1, 0:3], in0=scl_row[0:1, 0:3],
                                 scalar1=inv_n, scalar2=QUANT_EPS,
                                 op0=ALU.mult, op1=ALU.add)
            vector.drain()
            vector.tensor_scalar(out=sc_row12[0:1, 4:7], in0=sc_row12[0:1, 0:3],
                                 scalar1=0.5, scalar2=None, op0=ALU.mult)
            ins = vector.tensor_scalar(out=sc_row12[0:1, 8:11],
                                       in0=sc_row12[0:1, 0:3],
                                       scalar1=-0.5, scalar2=None, op0=ALU.mult)
            m_scrow.inc(ins, 1)

        @block.gpsimd
        def _(gpsimd):
            gpsimd.wait_ge(m_scrow.h, 1)
            gpsimd.dma_start(out=scl_bc, in_=sc_row12[0:1, :]).then_inc(m_gpio.h, 16)
            m_gpio.n += 16
            gpsimd.wait_ge(m_gpio.h, 48)
            gpsimd.dma_start(out=sc_b[:],
                             in_=scl_bc.to_broadcast((128, 12))).then_inc(m_gpio.h, 16)
            m_gpio.n += 16

        a1p.close()
        a1b = ExitStack()
        ps_b0 = psum(a1b, "ps_b0", [128, 512])
        ps_b1 = psum(a1b, "ps_b1", [128, 512])
        bc1tar = emit_bcast(T, sq_row, sum_row, r_b, nmr_b, r1tar, ps_b0, ps_b1)

        @block.sync
        def _(sync):
            for cc in range(CC):
                if cc < 2:
                    sync.wait_ge(m_xc.h, CC)
                    sync.wait_ge(m_xs.h, CC)
                else:
                    sync.wait_ge(m_n1.h, cc - 1)
                ins = sync.dma_start(out=xst[cc % 2][:],
                                     in_=x_in[cc * 128:(cc + 1) * 128, :])
                d_x2l.inc(ins, cc)

        @block.vector
        def _(vector):
            vector.wait_ge(m_bc.h, bc1tar)
            for cc in range(CC):
                d_x2l.wait_item(vector, (0, 0), cc)
                vector.tensor_tensor(out=xst[cc % 2][:], in0=xst[cc % 2][:],
                                     in1=r_b[:], op=ALU.mult)
                vector.drain()
                ins = vector.tensor_tensor(out=x_ln1[:, cc * T:(cc + 1) * T],
                                           in0=xst[cc % 2][:], in1=nmr_b[:],
                                           op=ALU.add)
                m_n1.inc(ins, 1)
        assert m_n1.n == CC

        @block.sync
        def _(sync):
            for i, (src, dst, r0, rows, c0, w, widx) in enumerate(tiles):
                reuse_wait(sync, m_q, 0, i)
                ins = sync.dma_start(out=qst[i % 2][0:rows, 0:w],
                                     in_=src[r0:r0 + rows, c0:c0 + w])
                d_sl2.inc(ins, i)

        NUG = sum(1 for t_ in tiles if t_[6] < 2)
        assert tiles[NUG - 1][6] < 2 and tiles[NUG][6] == 2
        RSB = m_sclmm.n
        RSEB = m_sclev.n

        @block.vector
        def _(vector):
            vector.wait_ge(m_gpio.h, 64)
            for i, (src, dst, r0, rows, c0, w, widx) in enumerate(tiles):
                buf, tbuf = qst[i % 2], tst[i % 4]
                d_sl2.wait_item(vector, (0, 0), i)
                if i >= 4:
                    d_tern.wait_item(vector, (0, 0), i - 4)
                    vector.wait_ge(m_sclmm.h, RSB + min(NUG, i - 3))
                vector.tensor_scalar(out=tbuf[0:rows, 0:w],
                                     in0=buf[0:rows, 0:w],
                                     scalar1=sc_b[0:rows, 8 + widx:9 + widx],
                                     scalar2=None, op0=ALU.is_le)
                vector.drain()
                ins = vector.scalar_tensor_tensor(
                    out=tbuf[0:rows, 0:w], in0=buf[0:rows, 0:w],
                    scalar=sc_b[0:rows, 4 + widx:5 + widx],
                    in1=tbuf[0:rows, 0:w], op0=ALU.is_ge, op1=ALU.subtract)
                m_q.inc(ins, 1)

        a1b.close()
        a1.close()
        a2 = ExitStack()
        wq_sl = [sb(a2, f"wq_sl{i}", [128, CC * 128], BF16) for i in range(3)]
        q_bf = [sb(a2, f"q_bf{i}", [128, T], BF16) for i in range(2)]
        ps_q = [psum(a2, f"ps_q{i}", [128, HALF]) for i in range(2)]
        NSRS = p.QW // 512
        ps_rs = [psum(a2, f"ps_rs{i}", [1, 512]) for i in range(NSRS)]
        assert NUG <= CC

        @block.gpsimd
        def _(gpsimd):
            gpsimd.wait_ge(m_n1.h, CC)
            for oc in range(CC):
                reuse_wait(gpsimd, m_qmm, 0, oc, unit=2, nbuf=3)
                if oc >= 2:
                    d_wq.wait_item(gpsimd, (0, 0), oc - 2)
                ins = gpsimd.dma_start(
                    out=wq_sl[oc % 3][:].rearrange("p (cc o) -> p cc o", cc=CC),
                    in_=wqkvT[:, oc * 128:(oc + 1) * 128].rearrange(
                        "(cc p) o -> p cc o", p=128))
                d_wq.inc(ins, oc)
                if oc == CC // 2:
                    kv_load(gpsimd, 0)

        @block.tensor
        def _(tensor):
            tensor.wait_ge(m_n1.h, CC)
            for oc in range(CC):
                d_wq.wait_item(tensor, (0, 0), oc)
                for th in range(2):
                    if oc >= 1:
                        tensor.wait_ge(m_qev.h, 2 * (oc - 1) + th + 1)
                    for cc in range(CC):
                        for (off, w) in _slices(HALF):
                            ins = tensor.matmul(
                                ps_q[th][:, off:off + w],
                                wq_sl[oc % 3][:, cc * 128:(cc + 1) * 128],
                                x_ln1[:, cc * T + th * HALF + off:
                                      cc * T + th * HALF + off + w],
                                start=(cc == 0), stop=(cc == CC - 1))
                    m_qmm.inc(ins, 1)
                if oc < NUG:
                    rows_, w_ = tiles[oc][3], tiles[oc][5]
                    tensor.wait_ge(m_q.h, oc + 1)
                    if oc >= 1:
                        tensor.wait_ge(m_sclev.h, RSEB + oc)
                    for si, (off, w2) in enumerate(_slices(w_)):
                        ins = tensor.matmul(ps_rs[si][0:1, 0:w2],
                                            ones_col_bf[0:rows_, :],
                                            tst[oc % 4][0:rows_, off:off + w2],
                                            start=True, stop=True)
                    m_sclmm.inc(ins, 1)
        assert m_qmm.n == 2 * CC and m_sclmm.n == RSB + NUG

        @block.scalar
        def _(scalar):
            for oc in range(CC):
                d_qsp.wait_prev(scalar, (0, 0), oc)
                for th in range(2):
                    scalar.wait_ge(m_qmm.h, 2 * oc + th + 1)
                    ins = scalar.activation(
                        q_bf[oc % 2][:, th * HALF:(th + 1) * HALF],
                        ps_q[th][:, 0:HALF], AF.Relu)
                    m_qev.inc(ins, 1)
                scalar.drain()
                ins = scalar.dma_start(out=q_sp[oc * 128:(oc + 1) * 128, :],
                                       in_=q_bf[oc % 2][:])
                d_qsp.inc(ins, oc)
                if oc < NUG:
                    (src, dst, r0, rows, c0, w, widx) = tiles[oc]
                    scalar.wait_ge(m_q.h, oc + 1)
                    if oc >= 2:
                        d_tern.wait_item(scalar, (0, 0), oc - 2)
                    ins = scalar.dma_start(out=dst[r0:r0 + rows, c0:c0 + w],
                                           in_=tst[oc % 4][0:rows, 0:w])
                    d_tern.inc(ins, oc)
                    scalar.wait_ge(m_sclmm.h, RSB + oc + 1)
                    rrow = widx * p.NRU + r0 // 128
                    hw_ = min(512, w)
                    nh = (w + hw_ - 1) // hw_
                    sls = _slices(w)
                    sph = hw_ // 512
                    for ih in range(nh):
                        if oc * nh + ih >= 1:
                            d_rs.wait_item(scalar, (0, 0), oc * nh + ih - 1)
                        for sj in range(sph):
                            si = ih * sph + sj
                            off, w2 = sls[si]
                            ins = scalar.copy(
                                rs_row[0:1, sj * 512:sj * 512 + w2],
                                ps_rs[si][0:1, 0:w2])
                        if ih == nh - 1:
                            m_sclev.inc(ins, 1)
                        scalar.drain()
                        ins = scalar.dma_start(
                            out=rsum_part[rrow:rrow + 1,
                                          c0 + ih * hw_:c0 + ih * hw_ + hw_],
                            in_=rs_row[0:1, 0:hw_])
                        d_rs.inc(ins, oc * nh + ih)

        @block.scalar
        def _(scalar):
            for i in range(NUG, NTL):
                (src, dst, r0, rows, c0, w, widx) = tiles[i]
                scalar.wait_ge(m_q.h, i + 1)
                if i >= 2:
                    d_tern.wait_item(scalar, (0, 0), i - 2)
                ins = scalar.dma_start(out=dst[r0:r0 + rows, c0:c0 + w],
                                       in_=tst[i % 4][0:rows, 0:w])
                d_tern.inc(ins, i)

        a2.close()
        sctx.close()
        a3 = ExitStack()
        wk_sl[1] = sb(a3, "wk_sl1", [128, CC * OGW], BF16)
        wv_sl[1] = sb(a3, "wv_sl1", [128, CC * OGW], BF16)
        VW = QG * 129
        k_tm = sb(a3, "k_tm", [128, TC * OGW], BF16)
        v_tm = sb(a3, "v_tm", [128, TC * OGW], BF16)
        ps_k = [psum(a3, f"ps_k{i}", [128, 512]) for i in range(2)]
        ps_v = [psum(a3, f"ps_v{i}", [128, 512]) for i in range(2)]
        ps_vk = [psum(a3, f"ps_vk{i}", [128, 512]) for i in range(QG)]

        @block.tensor
        def _(tensor):
            tensor.wait_ge(m_qev.h, 2 * CC)
            tensor.wait_ge(m_sclev.h, RSEB + NUG)

            def vk_group(it):
                og_, t_ = it // TC, it % TC
                tensor.wait_ge(m_kev.h, it + 1)
                tensor.wait_ge(m_vev.h, it + 1)
                if og_ >= 1 and t_ == 0:
                    tensor.wait_ge(m_vkms.h, og_ * QG)
                for qg in range(QG):
                    kt = k_tm[:, t_ * OGW + qg * 128:t_ * OGW + (qg + 1) * 128]
                    tensor.matmul(
                        ps_vk[qg][:, 0:128], kt,
                        v_tm[:, t_ * OGW + qg * 128:t_ * OGW + (qg + 1) * 128],
                        start=(t_ == 0), stop=False)
                    ins = tensor.matmul(
                        ps_vk[qg][:, 128:129], kt, ones_col_bf[:],
                        start=False, stop=(t_ == TC - 1))
                m_vkmm.inc(ins, 1)

            for og in range(NOG):
                d_wkv.wait_item(tensor, (0, 0), og)
                for t in range(TC):
                    it = og * TC + t
                    if it >= 2:
                        tensor.wait_ge(m_kev.h, it - 1)
                        tensor.wait_ge(m_vev.h, it - 1)
                    for cc in range(CC):
                        for (off, w) in _slices(OGW):
                            tensor.matmul(
                                ps_k[t % 2][:, off:off + w],
                                x_ln1[:, cc * T + t * 128:cc * T + (t + 1) * 128],
                                wk_sl[og % 2][:, cc * OGW + off:cc * OGW + off + w],
                                start=(cc == 0), stop=(cc == CC - 1))
                            ins = tensor.matmul(
                                ps_v[t % 2][:, off:off + w],
                                x_ln1[:, cc * T + t * 128:cc * T + (t + 1) * 128],
                                wv_sl[og % 2][:, cc * OGW + off:cc * OGW + off + w],
                                start=(cc == 0), stop=(cc == CC - 1))
                    m_kvmm.inc(ins, 1)
                    if it >= 1:
                        vk_group(it - 1)
            vk_group(NOG * TC - 1)

        @block.scalar
        def _(scalar):
            d_rs.wait_all(scalar)
            for og in range(NOG):
                for t in range(TC):
                    scalar.wait_ge(m_kvmm.h, og * TC + t + 1)
                    ins = scalar.activation(k_tm[:, t * OGW:(t + 1) * OGW],
                                            ps_k[t % 2][:, 0:OGW], AF.Relu)
                    m_kev.inc(ins, 1)

        @block.vector
        def _(vector):
            vector.wait_ge(m_const.h, 32)
            d_qsp.wait_all(vector)
            d_tern.wait_all(vector)
            d_rs.wait_all(vector)

            def vk_evac(og):
                vector.wait_ge(m_vkmm.h, (og + 1) * TC)
                for qg in range(QG):
                    g = og * QG + qg
                    ins = vector.tensor_tensor(
                        out=vk_sb[:, g * 129:(g + 1) * 129],
                        in0=ps_vk[qg][:, 0:129], in1=mask129_s[:], op=ALU.mult)
                    m_vkms.inc(ins, 1)

            for og in range(NOG):
                for t in range(TC):
                    vector.wait_ge(m_kvmm.h, og * TC + t + 1)
                    ins = vector.tensor_copy(
                        v_tm[:, t * OGW:(t + 1) * OGW],
                        ps_v[t % 2][:, 0:OGW])
                    m_vev.inc(ins, 1)
                    if og >= 1 and t == 1:
                        vk_evac(og - 1)
            vk_evac(NOG - 1)

        @block.scalar
        def _(scalar):
            scalar.wait_ge(m_vkms.h, G)
            scalar.dma_start(
                out=vk_part.rearrange("(g p) e -> p g e", p=128),
                in_=vk_sb[:].rearrange("p (g e) -> p g e", g=G)).then_inc(m_vkst.h, 16)
            m_vkst.n += 16

        a3.close()
        a3w.close()
        xln_ctx.close()
        b = ExitStack()
        bias_bf = sb(b, "bias_bf", [1, C], BF16)
        nst = sb(b, "nst", [128, G * 129], BF16)
        den_st = sb(b, "den_st", [128, G * 4], BF16)
        qh = [sb(b, f"qh{i}", [128, HALF], BF16) for i in range(4)]
        num_sb = [sb(b, f"num_sb{i}", [128, HALF], F32) for i in range(2)]
        rec_f = [sb(b, f"rec_f{i}", [4, HALF], F32) for i in range(2)]
        rec_bf = [sb(b, f"rec_bf{i}", [4, HALF], BF16) for i in range(2)]
        attn_bf = sb(b, "attn_bf", [128, G * HALF], BF16)
        wp_sl = [sb(b, f"wp_sl{i}", [128, CC * 128], BF16) for i in range(2)]
        y_f = sb(b, "y_f", [128, CC * HALF], F32)
        y_bf = [sb(b, f"y_bf{i}", [128, HALF], BF16) for i in range(2)]
        y_sq = [sb(b, f"y_sq{i}", [128, HALF], BF16) for i in range(2)]
        sum2 = sb(b, "sum2", [1, HALF], F32)
        sq2 = sb(b, "sq2", [1, HALF], F32)
        r2_row = sb(b, "r2_row", [1, HALF], F32)
        nmr2_row = sb(b, "nmr2_row", [1, HALF], F32)
        r2_b = sb(b, "r2_b", [128, HALF], F32)
        nmr2_b = sb(b, "nmr2_b", [128, HALF], F32)
        xres = [sb(b, f"xres{i}", [128, HALF], F32) for i in range(2)]
        x2t = [sb(b, f"x2t{i}", [128, HALF], F32) for i in range(2)]

        b_wp0 = d_wp.base()

        @block.gpsimd
        def _(gpsimd):
            d_tern.wait_all(gpsimd)
            d_qsp.wait_all(gpsimd)
            d_rs.wait_all(gpsimd)

            def og_load(og):
                reuse_wait(gpsimd, m_kvmm, 0, og, unit=TC)
                kv_load(gpsimd, og)

            if NOG > 1:
                og_load(1)
            ins = gpsimd.collective_compute(
                "AllReduce", ALU.add, replica_groups=all_cores,
                ins=[rsum_part.opt()], outs=[rsum_red.opt()])
            cc2.inc(ins, 1)
            ins = gpsimd.collective_compute(
                "AllGather", ALU.bypass, replica_groups=all_cores,
                ins=[tern_u_sl.opt()], outs=[tern_u.opt()])
            cc2.inc(ins, 1)
            if NOG > 2:
                og_load(2)
            ins = gpsimd.collective_compute(
                "AllGather", ALU.bypass, replica_groups=all_cores,
                ins=[tern_g_sl.opt()], outs=[tern_g.opt()])
            cc2.inc(ins, 1)
            for og in range(3, NOG):
                og_load(og)
            gpsimd.wait_ge(m_vkst.h, 16)
            ins = gpsimd.collective_compute(
                "AllReduce", ALU.add, replica_groups=batch_groups,
                ins=[vk_part.opt()], outs=[vk_red.opt()])
            cc3.inc(ins, 1)
            gpsimd.dma_start(out=bias_bf[:],
                             in_=bproj[:, :]).then_inc(m_gpc.h, 16)
            m_gpc.n += 16
            gpsimd.wait_ge(cc3.h, 1)
            gpsimd.dma_start(
                out=nst[:].rearrange("p (g e) -> p g e", g=G),
                in_=vk_red.rearrange("(g p) e -> p g e", p=128)).then_inc(
                    m_nst.h, 16)
            m_nst.n += 16
            for dc in range(2):
                ins = gpsimd.dma_start(
                    out=wp_sl[dc % 2][:].rearrange("p (cc o) -> p cc o", cc=CC),
                    in_=wprojT[:, dc * 128:(dc + 1) * 128].rearrange(
                        "(cc p) o -> p cc o", p=128))
                d_wp.inc(ins, dc)
            ins = gpsimd.collective_compute(
                "AllGather", ALU.bypass, replica_groups=all_cores,
                ins=[tern_d_sl.opt()], outs=[tern_d.opt()])
            cc2.inc(ins, 1)

        @block.vector
        def _(vector):
            vector.wait_ge(m_nst.h, 16)
            for g in range(G):
                ins = vector.tensor_tensor(
                    out=den_st[:, g * 4:(g + 1) * 4],
                    in0=nst[:, g * 129 + 128:g * 129 + 129].to_broadcast((128, 4)),
                    in1=bm4_s[:], op=ALU.mult)
            m_den.inc(ins, 1)

        for h in range(2):
            bp = ExitStack()
            ps_num = psum(bp, "ps_num", [128, HALF])
            ps_den = psum(bp, "ps_den", [4, HALF])
            ps_div = psum(bp, "ps_div", [128, HALF])

            b_qld = d_qld.base()
            NB, TB = m_bnum.n, m_bts.n
            RB, DB, EVB, AB = m_brec.n, m_bdiv.n, m_bnev.n, m_batn.n
            BCB = m_bc.n

            @block.sync
            def _(sync):
                if h == 0:
                    d_qsp.wait_all(sync)
                    sync.wait_ge(m_kvmm.h, NOG * TC)
                    sync.wait_ge(m_vkmm.h, NOG * TC)
                    sync.wait_ge(m_vkst.h, 16)
                for g in range(G):
                    reuse_wait(sync, m_bnum, NB, g, nbuf=4)
                    if g >= 2:
                        d_qld.wait_item(sync, b_qld, g - 2)
                    ins = sync.dma_start(
                        out=qh[g % 4][:],
                        in_=q_sp[g * 128:(g + 1) * 128, h * HALF:(h + 1) * HALF])
                    d_qld.inc(ins, g)

            @block.tensor
            def _(tensor):
                tensor.wait_ge(m_den.h, 1)
                tensor.wait_ge(m_gpc.h, 32)
                if h == 1:
                    tensor.wait_ge(m_bc.h, BCB)

                def div_mm(j):
                    tensor.wait_ge(m_brec.h, RB + j + 1)
                    reuse_wait(tensor, m_batn, AB, j, nbuf=1)
                    for (off, w) in _slices(HALF):
                        ins = tensor.matmul(ps_div[:, off:off + w], p4_bf[:],
                                            rec_bf[j % 2][:, off:off + w],
                                            start=True, stop=True)
                    m_bdiv.inc(ins, 1)

                for g in range(G):
                    d_qld.wait_item(tensor, b_qld, g)
                    reuse_wait(tensor, m_bnev, EVB, g, nbuf=1)
                    reuse_wait(tensor, m_bts, TB, g, nbuf=1)
                    for (off, w) in _slices(HALF):
                        tensor.matmul(ps_num[:, off:off + w],
                                      nst[:, g * 129:g * 129 + 128],
                                      qh[g % 4][:, off:off + w],
                                      start=True, stop=True)
                        ins = tensor.matmul(ps_den[:, off:off + w],
                                            den_st[:, g * 4:(g + 1) * 4],
                                            qh[g % 4][:, off:off + w],
                                            start=True, stop=True)
                    m_bnum.inc(ins, 1)
                    if g >= 1:
                        div_mm(g - 1)
                div_mm(G - 1)

            @block.vector
            def _(vector):
                if h == 1:
                    vector.wait_ge(m_pmm.h, CC)
                for g in range(G):
                    vector.wait_ge(m_bnum.h, NB + g + 1)
                    reuse_wait(vector, m_bdiv, DB, g, nbuf=2)
                    vector.tensor_scalar(out=rec_f[g % 2][:, 0:HALF],
                                         in0=ps_den[:, 0:HALF],
                                         scalar1=ATTN_EPS, scalar2=None,
                                         op0=ALU.add)
                    ins = vector.drain()
                    m_bts.inc(ins, 1)
                    with nc.allow_low_precision(
                            reason="bf16 reciprocal feeds a bf16 matmul"):
                        ins = vector.reciprocal(out=rec_bf[g % 2][:, 0:HALF],
                                                in_=rec_f[g % 2][:, 0:HALF])
                    m_brec.inc(ins, 1)
                    vector.wait_ge(m_bnev.h, EVB + g + 1)
                    vector.wait_ge(m_bdiv.h, DB + g + 1)
                    ins = vector.tensor_tensor(
                        out=attn_bf[:, g * HALF:(g + 1) * HALF],
                        in0=num_sb[g % 2][:], in1=ps_div[:, 0:HALF], op=ALU.mult)
                    m_batn.inc(ins, 1)

            @block.scalar
            def _(scalar):
                for g in range(G):
                    scalar.wait_ge(m_bnum.h, NB + g + 1)
                    reuse_wait(scalar, m_batn, AB, g)
                    ins = scalar.copy(num_sb[g % 2][:], ps_num[:, 0:HALF])
                    m_bnev.inc(ins, 1)

            bp.close()

            b_wp = b_wp0 if h == 0 else d_wp.base()
            PB, PEB = m_pmm.n, m_pev.n
            YB, YS, SB2 = m_ybf.n, m_ysq.n, m_st2.n
            V2B = m_var.n

            @block.gpsimd
            def _(gpsimd):
                for dc in range(2 if h == 0 else 0, CC):
                    reuse_wait(gpsimd, m_pmm, PB, dc)
                    ins = gpsimd.dma_start(
                        out=wp_sl[dc % 2][:].rearrange("p (cc o) -> p cc o", cc=CC),
                        in_=wprojT[:, dc * 128:(dc + 1) * 128].rearrange(
                            "(cc p) o -> p cc o", p=128))
                    d_wp.inc(ins, dc)

            bps = ExitStack()
            ps_pj = [psum(bps, f"ps_pj{i}", [128, HALF]) for i in range(2)]
            ps_sum2 = psum(bps, "ps_sum2", [1, HALF])
            ps_sq2 = psum(bps, "ps_sq2", [1, HALF])

            def b_stats(tensor, dc):
                tensor.wait_ge(m_ybf.h, YB + dc + 1)
                tensor.wait_ge(m_ysq.h, YS + dc + 1)
                for (off, w) in _slices(HALF):
                    tensor.matmul(ps_sum2[0:1, off:off + w], ones_col_bf[:],
                                  y_bf[dc % 2][:, off:off + w],
                                  start=(dc == 0), stop=(dc == CC - 1))
                    ins = tensor.matmul(ps_sq2[0:1, off:off + w], ones_col_bf[:],
                                        y_sq[dc % 2][:, off:off + w],
                                        start=(dc == 0), stop=(dc == CC - 1))
                m_st2.inc(ins, 1)

            @block.tensor
            def _(tensor):
                tensor.wait_ge(m_batn.h, AB + G)
                if h == 1:
                    tensor.wait_ge(m_var.h, V2B)
                for dc in range(CC):
                    d_wp.wait_item(tensor, b_wp, dc)
                    reuse_wait(tensor, m_pev, PEB, dc)
                    for cc in range(CC):
                        for (off, w) in _slices(HALF):
                            tensor.matmul(
                                ps_pj[dc % 2][:, off:off + w],
                                wp_sl[dc % 2][:, cc * 128:(cc + 1) * 128],
                                attn_bf[:, cc * HALF + off:cc * HALF + off + w],
                                start=(cc == 0), stop=False)
                    for (off, w) in _slices(HALF):
                        ins = tensor.matmul(
                            ps_pj[dc % 2][:, off:off + w],
                            bias_bf[0:1, dc * 128:(dc + 1) * 128],
                            ones_row_bf[0:1, 0:w],
                            start=False, stop=True)
                    m_pmm.inc(ins, 1)
                    if dc >= 1:
                        b_stats(tensor, dc - 1)
                b_stats(tensor, CC - 1)

            @block.scalar
            def _(scalar):
                if h == 1:
                    scalar.wait_ge(m_x2n.h, CC)
                    scalar.wait_ge(m_ybf.h, CC)
                for dc in range(CC):
                    scalar.wait_ge(m_pmm.h, PB + dc + 1)
                    ins = scalar.copy(y_f[:, dc * HALF:(dc + 1) * HALF],
                                      ps_pj[dc % 2][:, 0:HALF])
                    m_pev.inc(ins, 1)
                    scalar.drain()
                    if dc >= 2:
                        scalar.wait_ge(m_st2.h, SB2 + dc - 1)
                    ins = scalar.activation(y_sq[dc % 2][:],
                                            y_f[:, dc * HALF:(dc + 1) * HALF],
                                            AF.Square)
                    m_ysq.inc(ins, 1)

            @block.vector
            def _(vector):
                for dc in range(CC):
                    vector.wait_ge(m_ysq.h, YS + dc + 1)
                    if dc >= 2:
                        vector.wait_ge(m_st2.h, SB2 + dc - 1)
                    ins = vector.tensor_copy(y_bf[dc % 2][:],
                                             y_f[:, dc * HALF:(dc + 1) * HALF])
                    m_ybf.inc(ins, 1)

            r2tar = emit_rows(HALF, ps_sum2, ps_sq2, m_st2, SB2 + CC,
                              sum2, sq2, r2_row, nmr2_row)
            bps.close()
            bbc = ExitStack()
            ps_c0 = psum(bbc, "ps_c0", [128, 512])
            ps_c1 = psum(bbc, "ps_c1", [128, 512])
            bc2tar = emit_bcast(HALF, r2_row, nmr2_row, r2_b, nmr2_b,
                                r2tar, ps_c0, ps_c1,
                                extra_pe_waits=((m_pev, PEB + CC),))

            b_xrl = d_xrl.base()
            b_x2st = d_x2st.base()
            NB2 = m_x2n.n

            @block.sync
            def _(sync):
                for cc in range(CC):
                    reuse_wait(sync, m_x2n, NB2, cc)
                    ins = sync.dma_start(
                        out=xres[cc % 2][:],
                        in_=x_in[cc * 128:(cc + 1) * 128, h * HALF:(h + 1) * HALF])
                    d_xrl.inc(ins, cc)

            @block.vector
            def _(vector):
                vector.wait_ge(m_bc.h, bc2tar)
                for cc in range(CC):
                    d_xrl.wait_item(vector, b_xrl, cc)
                    d_x2st.wait_prev(vector, b_x2st, cc)
                    vector.tensor_tensor(out=x2t[cc % 2][:],
                                         in0=y_f[:, cc * HALF:(cc + 1) * HALF],
                                         in1=r2_b[:], op=ALU.mult)
                    vector.drain()
                    vector.tensor_tensor(out=x2t[cc % 2][:], in0=x2t[cc % 2][:],
                                         in1=nmr2_b[:], op=ALU.add)
                    vector.drain()
                    ins = vector.tensor_tensor(out=x2t[cc % 2][:],
                                               in0=x2t[cc % 2][:],
                                               in1=xres[cc % 2][:], op=ALU.add)
                    m_x2n.inc(ins, 1)

            @block.scalar
            def _(scalar):
                for cc in range(CC):
                    scalar.wait_ge(m_x2n.h, NB2 + cc + 1)
                    ins = scalar.dma_start(
                        out=x2_sp[cc * 128:(cc + 1) * 128, h * HALF:(h + 1) * HALF],
                        in_=x2t[cc % 2][:])
                    d_x2st.inc(ins, cc)

            bbc.close()

        b.close()
        c = ExitStack()
        EG, DG, EBC = p.EG, p.DG, p.EB
        NEG = ECH // EG
        NDG = CC // DG
        NEB = ECH // EBC
        CPD = CC // NDG
        assert CPD * NDG == CC and NDG * DG == CC
        x2q_bf = [sb(c, f"x2q_bf{i}", [128, CC * CQ], BF16) for i in range(2)]
        x2f = [sb(c, f"x2f{i}", [128, CQ], F32) for i in range(2)]
        x2sq = [sb(c, f"x2sq{i}", [128, CQ], BF16) for i in range(2)]
        sum3 = sb(c, "sum3", [1, CQ], F32)
        sq3 = sb(c, "sq3", [1, CQ], F32)
        r3_row = sb(c, "r3_row", [1, CQ], F32)
        nmr3_row = sb(c, "nmr3_row", [1, CQ], F32)
        r3_b = sb(c, "r3_b", [128, CQ], F32)
        nmr3_b = sb(c, "nmr3_b", [128, CQ], F32)
        rsB = sb(c, "rsB", [ECH, 2 * p.NRU * 128], F32)
        id_s = sb(c, "id_s", [128, 128], F32)
        rsumT_u = sb(c, "rsumT_u", [128, ECH], F32)
        rsumT_g = sb(c, "rsumT_g", [128, ECH], F32)
        hid = sb(c, "hid", [128, ECH * CQ], BF16)
        wu_sl = [sb(c, f"wu_sl{i}", [128, CC * EG * 128], BF16) for i in range(2)]
        wg_sl = [sb(c, f"wg_sl{i}", [128, CC * EG * 128], BF16) for i in range(2)]
        au_t = [sb(c, f"au_t{i}", [128, CQ], F32) for i in range(2)]
        gt_t = [sb(c, f"gt_t{i}", [128, CQ], F32) for i in range(2)]
        silu_f = [sb(c, f"silu_f{i}", [128, CQ], F32) for i in range(2)]
        wd_sl = [sb(c, f"wd_sl{i}", [128, EBC * DG * 128], BF16) for i in range(4)]
        y4_f = [sb(c, f"y4_f{i}", [128, CQ], F32) for i in range(2)]
        y4bf = [sb(c, f"y4bf{i}", [128, CQ], BF16) for i in range(2)]
        y4sq = [sb(c, f"y4sq{i}", [128, CQ], BF16) for i in range(2)]
        sum4 = sb(c, "sum4", [1, CQ], F32)
        sq4 = sb(c, "sq4", [1, CQ], F32)
        r4_row = sb(c, "r4_row", [1, CQ], F32)
        nmr4_row = sb(c, "nmr4_row", [1, CQ], F32)
        r4_b = sb(c, "r4_b", [128, CQ], F32)
        nmr4_b = sb(c, "nmr4_b", [128, CQ], F32)
        x2st_ = [sb(c, f"x2st{i}", [128, CQ], F32) for i in range(3)]
        y4res = [sb(c, f"y4res{i}", [128, CQ], F32) for i in range(3)]
        outt = [sb(c, f"outt{i}", [128, CQ], F32) for i in range(2)]
        ps_up = [psum(c, f"ps_up{i}", [128, CQ]) for i in range(2)]
        ps_gt = [psum(c, f"ps_gt{i}", [128, CQ]) for i in range(2)]
        ps_dn = [psum(c, f"ps_dn{i}", [128, CQ]) for i in range(2)]
        ps_sum4 = psum(c, "ps_sum4", [1, CQ])
        ps_sq4 = psum(c, "ps_sq4", [1, CQ])

        NRW_ = 2 * p.NRU

        @block.gpsimd
        def _(gpsimd):
            gpsimd.wait_ge(cc2.h, 1)
            d_x2st.wait_all(gpsimd)
            ins = gpsimd.dma_start(out=id_s[:], in_=id128[:, :])
            m_gpio.inc(ins, 16)
            for rr_ in range(NRW_):
                ins = gpsimd.dma_start(
                    out=rsB[:, rr_ * 128:(rr_ + 1) * 128],
                    in_=rsum_red[rr_:rr_ + 1, :].rearrange(
                        "one (pp x) -> pp (one x)", pp=ECH))
                m_gpio.inc(ins, 16)
        RSLT = m_gpio.n

        @block.tensor
        def _(tensor):
            tensor.wait_ge(m_gpio.h, RSLT)
            tensor.wait_ge(m_bc.h, m_bc.n)
            for rr_ in range(NRW_):
                ins = tensor.matmul(ps_dn[0][:, rr_ * ECH:(rr_ + 1) * ECH],
                                    rsB[0:ECH, rr_ * 128:(rr_ + 1) * 128],
                                    id_s[0:ECH, 0:ECH],
                                    start=(rr_ == 0), stop=(rr_ == NRW_ - 1))
            m_sclmm.inc(ins, 1)
        RSMM_T = m_sclmm.n

        @block.vector
        def _(vector):
            vector.wait_ge(m_sclmm.h, RSMM_T)
            for (dst_, b0_) in ((rsumT_u, 0), (rsumT_g, p.NRU)):
                ins = vector.tensor_copy(
                    dst_[:], ps_dn[0][:, b0_ * ECH:(b0_ + 1) * ECH])
                if p.NRU > 1:
                    assert p.NRU == 2
                    vector.drain()
                    ins = vector.tensor_tensor(
                        out=dst_[:], in0=dst_[:],
                        in1=ps_dn[0][:, (b0_ + 1) * ECH:(b0_ + 2) * ECH],
                        op=ALU.add)
            m_sclev.inc(ins, 1)
            vector.drain()
        RSDONE = m_sclev.n

        b_x3l = d_x3l.base()
        XC0, XS0, ST3P = m_x3c.n, m_x3s.n, m_st3.n

        @block.gpsimd
        def _(gpsimd):
            for cc in range(CC):
                if cc >= 2:
                    gpsimd.wait_ge(m_x3c.h, XC0 + cc - 1)
                    gpsimd.wait_ge(m_x3s.h, XS0 + cc - 1)
                ins = gpsimd.dma_start(out=x2f[cc % 2][:],
                                       in_=x2_sp[cc * 128:(cc + 1) * 128,
                                                 0:CQ])
                d_x3l.inc(ins, cc)

        @block.vector
        def _(vector):
            for cc in range(CC):
                d_x3l.wait_item(vector, b_x3l, cc)
                ins = vector.tensor_copy(x2q_bf[0][:, cc * CQ:(cc + 1) * CQ],
                                         x2f[cc % 2][:])
                m_x3c.inc(ins, 1)

        @block.scalar
        def _(scalar):
            for cc in range(CC):
                d_x3l.wait_item(scalar, b_x3l, cc)
                reuse_wait(scalar, m_st3, ST3P, cc)
                ins = scalar.activation(x2sq[cc % 2][:], x2f[cc % 2][:],
                                        AF.Square)
                m_x3s.inc(ins, 1)

        @block.tensor
        def _(tensor):
            tensor.wait_ge(m_bc.h, m_bc.n)
            for cc in range(CC):
                tensor.wait_ge(m_x3c.h, XC0 + cc + 1)
                tensor.wait_ge(m_x3s.h, XS0 + cc + 1)
                for (off, w) in _slices(CQ):
                    tensor.matmul(ps_up[0][0:1, off:off + w], ones_col_bf[:],
                                  x2q_bf[0][:, cc * CQ + off:cc * CQ + off + w],
                                  start=(cc == 0), stop=(cc == CC - 1))
                    ins = tensor.matmul(ps_gt[0][0:1, off:off + w],
                                        ones_col_bf[:],
                                        x2sq[cc % 2][:, off:off + w],
                                        start=(cc == 0), stop=(cc == CC - 1))
                m_st3.inc(ins, 1)

        r3tar = emit_rows(CQ, ps_up[0], ps_gt[0], m_st3, ST3P + CC,
                          sum3, sq3, r3_row, nmr3_row)
        v3guard = m_var.n
        bc3tar = emit_bcast(CQ, r3_row, nmr3_row, r3_b, nmr3_b, r3tar,
                            ps_up[1], ps_gt[1])
        bc4tar_prev = m_bc.n

        for qq in range(NQ):
            q0 = qq * CQ
            last = (qq == NQ - 1)
            if qq == 0:
                b_wug = d_wug.base()
            UP, GT = m_upmm.n, m_gtmm.n
            SI, HD = m_silu.n, m_hid.n
            EV = m_ev.n

            def wug_load(gpsimd, eg):
                e0 = eg * EG * 128
                ew = EG * 128
                ins = gpsimd.dma_start(
                    out=wu_sl[eg % 2][:].rearrange("p (cc e) -> p cc e", cc=CC),
                    in_=tern_u[:, e0:e0 + ew].rearrange(
                        "(cc p) e -> p cc e", p=128))
                d_wug.inc(ins, eg)
                ins = gpsimd.dma_start(
                    out=wg_sl[eg % 2][:].rearrange("p (cc e) -> p cc e", cc=CC),
                    in_=tern_g[:, e0:e0 + ew].rearrange(
                        "(cc p) e -> p cc e", p=128))
                d_wug.inc(ins, eg)

            @block.gpsimd
            def _(gpsimd):
                if qq == 0:
                    gpsimd.wait_ge(cc2.h, 4)
                    d_x2st.wait_all(gpsimd)
                for eg in range(min(2, NEG) if qq >= 1 else 0, NEG):
                    reuse_wait(gpsimd, m_gtmm, GT, eg, unit=EG)
                    wug_load(gpsimd, eg)

            @block.tensor
            def _(tensor):
                tensor.wait_ge(m_x3c.h, m_x3c.n)
                tensor.wait_ge(m_var.h, v3guard)
                for ec in range(ECH):
                    eg, el = ec // EG, ec % EG
                    d_wug.wait_item(tensor, b_wug, eg)
                    if ec == 1:
                        tensor.wait_ge(m_bc.h, bc3tar)
                    if ec >= 2:
                        tensor.wait_ge(m_ev.h, EV + 3 * (ec - 1))
                    elif EV > 0:
                        tensor.wait_ge(m_ev.h, EV)
                    for cc in range(CC):
                        for (off, w) in _slices(CQ):
                            ins = tensor.matmul(
                                ps_up[ec % 2][:, off:off + w],
                                wu_sl[eg % 2][:, cc * EG * 128 + el * 128:
                                              cc * EG * 128 + (el + 1) * 128],
                                x2q_bf[qq % 2][:, cc * CQ + off:
                                               cc * CQ + off + w],
                                start=(cc == 0), stop=(cc == CC - 1))
                    m_upmm.inc(ins, 1)
                    for cc in range(CC):
                        for (off, w) in _slices(CQ):
                            ins = tensor.matmul(
                                ps_gt[ec % 2][:, off:off + w],
                                wg_sl[eg % 2][:, cc * EG * 128 + el * 128:
                                              cc * EG * 128 + (el + 1) * 128],
                                x2q_bf[qq % 2][:, cc * CQ + off:
                                               cc * CQ + off + w],
                                start=(cc == 0), stop=(cc == CC - 1))
                    m_gtmm.inc(ins, 1)

            @block.vector
            def _(vector):
                if qq >= 1:
                    vector.wait_ge(m_dnsl.h, qq * NDG * NEB)
                vector.wait_ge(m_bc.h, bc3tar)
                for ec in range(ECH):
                    vector.wait_ge(m_upmm.h, UP + ec + 1)
                    reuse_wait(vector, m_silu, SI, ec)
                    ins = vector.tensor_tensor(out=au_t[ec % 2][:],
                                               in0=ps_up[ec % 2][:, 0:CQ],
                                               in1=r3_b[:], op=ALU.mult)
                    m_ev.inc(ins, 1)
                    vector.drain()
                    ins = vector.scalar_tensor_tensor(
                        out=au_t[ec % 2][:], in0=nmr3_b[:],
                        scalar=rsumT_u[:, ec:ec + 1],
                        in1=au_t[ec % 2][:], op0=ALU.mult, op1=ALU.add)
                    m_ev.inc(ins, 1)
                    vector.wait_ge(m_gtmm.h, GT + ec + 1)
                    ins = vector.tensor_tensor(out=gt_t[ec % 2][:],
                                               in0=ps_gt[ec % 2][:, 0:CQ],
                                               in1=r3_b[:], op=ALU.mult)
                    m_ev.inc(ins, 1)
                    vector.drain()
                    vector.scalar_tensor_tensor(
                        out=gt_t[ec % 2][:], in0=nmr3_b[:],
                        scalar=rsumT_g[:, ec:ec + 1],
                        in1=gt_t[ec % 2][:], op0=ALU.mult, op1=ALU.add)
                    if ec >= 1:
                        vector.wait_ge(m_silu.h, SI + ec)
                        vector.drain()
                        ins = vector.scalar_tensor_tensor(
                            out=hid[:, (ec - 1) * CQ:ec * CQ],
                            in0=gt_t[(ec - 1) % 2][:], scalar=sc_b[:, 1:2],
                            in1=silu_f[(ec - 1) % 2][:],
                            op0=ALU.mult, op1=ALU.mult)
                        m_hid.inc(ins, 1)
                vector.wait_ge(m_silu.h, SI + ECH)
                vector.drain()
                ins = vector.scalar_tensor_tensor(
                    out=hid[:, (ECH - 1) * CQ:ECH * CQ],
                    in0=gt_t[(ECH - 1) % 2][:], scalar=sc_b[:, 1:2],
                    in1=silu_f[(ECH - 1) % 2][:], op0=ALU.mult, op1=ALU.mult)
                m_hid.inc(ins, 1)

            @block.scalar
            def _(scalar):
                for ec in range(ECH):
                    scalar.wait_ge(m_ev.h, EV + 3 * ec + 2)
                    reuse_wait(scalar, m_hid, HD, ec)
                    ins = scalar.activation(silu_f[ec % 2][:], au_t[ec % 2][:],
                                            AF.Silu, scale=sc_b[:, 0:1])
                    m_silu.inc(ins, 1)

            b_wd = d_wd.base()
            b_wd2 = d_wd2.base()
            b_x3l = d_x3l.base()
            b_y4st = d_y4st.base()
            DSL, DE = m_dnsl.n, m_dnev.n
            Y4C, Y4S, ST4 = m_y4c.n, m_y4s.n, m_st4.n
            XC, XS3, ST3 = m_x3c.n, m_x3s.n, m_st3.n
            V4B = m_var.n

            nwd = NDG * NEB

            def wd_load(eng, stream, i):
                reuse_wait(eng, m_dnsl, DSL, i, nbuf=4)
                eb = i % NEB
                dg = i // NEB
                e0 = eb * EBC * 128
                d0 = dg * DG * 128
                ins = eng.dma_start(
                    out=wd_sl[i % 4][:].rearrange(
                        "p (es d) -> p es d", es=EBC),
                    in_=tern_d[e0:e0 + EBC * 128,
                               d0:d0 + DG * 128].rearrange(
                        "(es p) d -> p es d", p=128))
                stream.inc(ins, i // 2)

            @block.sync
            def _(sync):
                if qq == 0:
                    d_x2st.wait_all(sync)
                    sync.wait_ge(cc2.h, 4)
                for i in range(0, nwd, 2):
                    wd_load(sync, d_wd, i)

            if not last:
                b_wug_next = d_wug.base()

                @block.gpsimd
                def _(gpsimd):
                    gpsimd.wait_ge(m_gtmm.h, m_gtmm.n)
                    for eg in range(min(2, NEG)):
                        wug_load(gpsimd, eg)
                    for cc in range(CC):
                        if cc >= 2:
                            gpsimd.wait_ge(m_x3c.h, XC + cc - 1)
                            gpsimd.wait_ge(m_x3s.h, XS3 + cc - 1)
                        elif XC > 0:
                            gpsimd.wait_ge(m_x3c.h, XC)
                            gpsimd.wait_ge(m_x3s.h, XS3)
                        ins = gpsimd.dma_start(
                            out=x2f[cc % 2][:],
                            in_=x2_sp[cc * 128:(cc + 1) * 128,
                                      q0 + CQ:q0 + 2 * CQ])
                        d_x3l.inc(ins, cc)

            def c_stats(tensor, dc):
                tensor.wait_ge(m_y4c.h, Y4C + dc + 1)
                tensor.wait_ge(m_y4s.h, Y4S + dc + 1)
                for (off, w) in _slices(CQ):
                    tensor.matmul(ps_sum4[0:1, off:off + w], ones_col_bf[:],
                                  y4bf[dc % 2][:, off:off + w],
                                  start=(dc == 0), stop=(dc == CC - 1))
                    ins = tensor.matmul(ps_sq4[0:1, off:off + w],
                                        ones_col_bf[:],
                                        y4sq[dc % 2][:, off:off + w],
                                        start=(dc == 0), stop=(dc == CC - 1))
                m_st4.inc(ins, 1)

            def c_stats3(tensor, cc):
                if cc == 0:
                    tensor.wait_ge(m_ev.h, m_ev.n)
                tensor.wait_ge(m_x3c.h, XC + cc + 1)
                tensor.wait_ge(m_x3s.h, XS3 + cc + 1)
                for (off, w) in _slices(CQ):
                    tensor.matmul(ps_up[0][0:1, off:off + w], ones_col_bf[:],
                                  x2q_bf[(qq + 1) % 2][:, cc * CQ + off:
                                                       cc * CQ + off + w],
                                  start=(cc == 0), stop=(cc == CC - 1))
                    ins = tensor.matmul(ps_gt[0][0:1, off:off + w],
                                        ones_col_bf[:],
                                        x2sq[cc % 2][:, off:off + w],
                                        start=(cc == 0), stop=(cc == CC - 1))
                m_st3.inc(ins, 1)

            @block.tensor
            def _(tensor):
                tensor.wait_ge(m_hid.h, HD + ECH)
                tensor.wait_ge(m_bc.h, bc4tar_prev)
                if qq == 0:
                    tensor.wait_ge(m_sclev.h, RSDONE)
                if qq >= 1:
                    tensor.wait_ge(m_var.h, V4B)
                for dg in range(NDG):
                    if dg >= 1:
                        tensor.wait_ge(m_dnev.h, DE + dg * DG)
                    for eb in range(NEB):
                        i = dg * NEB + eb
                        if i % 2 == 0:
                            d_wd.wait_item(tensor, b_wd, i // 2)
                        else:
                            d_wd2.wait_item(tensor, b_wd2, i // 2)
                        for dl in range(DG):
                            for el in range(EBC):
                                ec = eb * EBC + el
                                for (off, w) in _slices(CQ):
                                    ins = tensor.matmul(
                                        ps_dn[dl][:, off:off + w],
                                        wd_sl[i % 4][:, el * DG * 128 + dl * 128:
                                                      el * DG * 128 + (dl + 1) * 128],
                                        hid[:, ec * CQ + off:ec * CQ + off + w],
                                        start=(eb == 0 and el == 0),
                                        stop=(eb == NEB - 1 and el == EBC - 1))
                        m_dnsl.inc(ins, 1)
                    if dg >= 1:
                        for dl in range(DG):
                            c_stats(tensor, (dg - 1) * DG + dl)
                    if not last:
                        nd2 = max(1, NDG // 2)
                        cpd2 = CC // nd2
                        if dg < nd2:
                            for k_ in range(cpd2):
                                c_stats3(tensor, dg * cpd2 + k_)
                for dl in range(DG):
                    c_stats(tensor, (NDG - 1) * DG + dl)

            @block.scalar
            def _(scalar):
                if qq >= 1:
                    scalar.wait_ge(m_y4c.h, Y4C)
                nsq = [0]
                for dg in range(NDG):
                    for i in range(dg * NEB + 1, (dg + 1) * NEB, 2):
                        wd_load(scalar, d_wd2, i)
                    for dl in range(DG):
                        dc = dg * DG + dl
                        if not last:
                            while nsq[0] < CC and nsq[0] <= (dc + 1) * 2 - 1:
                                cc = nsq[0]
                                d_x3l.wait_item(scalar, b_x3l, cc)
                                reuse_wait(scalar, m_st3, ST3, cc)
                                ins = scalar.activation(x2sq[cc % 2][:],
                                                        x2f[cc % 2][:],
                                                        AF.Square)
                                m_x3s.inc(ins, 1)
                                nsq[0] += 1
                        scalar.wait_ge(m_dnsl.h, DSL + (dg + 1) * NEB)
                        d_y4st.wait_prev(scalar, b_y4st, dc)
                        reuse_wait(scalar, m_y4c, Y4C, dc)
                        reuse_wait(scalar, m_st4, ST4, dc)
                        ins = scalar.mul(y4_f[dc % 2][:], ps_dn[dl][:, 0:CQ],
                                         sc_b[:, 2:3])
                        m_dnev.inc(ins, 1)
                        scalar.drain()
                        ins = scalar.activation(y4sq[dc % 2][:], y4_f[dc % 2][:],
                                                AF.Square)
                        m_y4s.inc(ins, 1)
                        ins = scalar.dma_start(
                            out=y4_sp[dc * 128:(dc + 1) * 128, q0:q0 + CQ],
                            in_=y4_f[dc % 2][:])
                        d_y4st.inc(ins, dc)

            @block.vector
            def _(vector):
                ncst = [0]
                for dc in range(CC):
                    if not last:
                        while ncst[0] < CC and ncst[0] <= (dc + 1) * 2 - 1:
                            cc = ncst[0]
                            d_x3l.wait_item(vector, b_x3l, cc)
                            ins = vector.tensor_copy(
                                x2q_bf[(qq + 1) % 2][:, cc * CQ:(cc + 1) * CQ],
                                x2f[cc % 2][:])
                            m_x3c.inc(ins, 1)
                            ncst[0] += 1
                    vector.wait_ge(m_y4s.h, Y4S + dc + 1)
                    reuse_wait(vector, m_st4, ST4, dc)
                    ins = vector.tensor_copy(y4bf[dc % 2][:], y4_f[dc % 2][:])
                    m_y4c.inc(ins, 1)

            if not last:
                r3tar = emit_rows(CQ, ps_up[0], ps_gt[0], m_st3, ST3 + CC,
                                  sum3, sq3, r3_row, nmr3_row)
                v3guard = m_var.n
            r4tar = emit_rows(CQ, ps_sum4, ps_sq4, m_st4, ST4 + CC,
                              sum4, sq4, r4_row, nmr4_row)
            if not last:
                bc3tar = emit_bcast(CQ, r3_row, nmr3_row, r3_b, nmr3_b, r3tar,
                                    ps_up[1], ps_gt[1],
                                    extra_pe_waits=((m_ev, m_ev.n),))
            bc4tar = emit_bcast(CQ, r4_row, nmr4_row, r4_b, nmr4_b, r4tar,
                                ps_dn[0], ps_dn[1],
                                extra_pe_waits=((m_dnev, DE + NDG * DG),
                                                (m_on, m_on.n),
                                                (m_onp, m_onp.n)))
            bc4tar_prev = bc4tar

            b_xfl = d_xfl.base()
            b_yfl = d_yfl.base()
            b_out = d_out.base()
            ON, ONP = m_on.n, m_onp.n

            @block.sync
            def _(sync):
                for cc in range(CC):
                    if cc >= 3:
                        j = cc - 3
                        if j % 2 == 0:
                            sync.wait_ge(m_on.h, ON + j // 2 + 1)
                        else:
                            sync.wait_ge(m_onp.h, ONP + (j + 1) // 2)
                    else:
                        if ON > 0:
                            sync.wait_ge(m_on.h, ON)
                        if ONP > 0:
                            sync.wait_ge(m_onp.h, ONP)
                    if cc >= 2:
                        d_xfl.wait_item(sync, b_xfl, cc - 2)
                        d_yfl.wait_item(sync, b_yfl, cc - 2)
                    ins = sync.dma_start(
                        out=x2st_[cc % 3][:],
                        in_=x2_sp[cc * 128:(cc + 1) * 128, q0:q0 + CQ])
                    d_xfl.inc(ins, cc)
                    d_y4st.wait_item(sync, b_y4st, cc)
                    ins = sync.dma_start(
                        out=y4res[cc % 3][:],
                        in_=y4_sp[cc * 128:(cc + 1) * 128, q0:q0 + CQ])
                    d_yfl.inc(ins, cc)

            @block.vector
            def _(vector):
                vector.wait_ge(m_bc.h, bc4tar)
                for cc in range(0, CC, 2):
                    d_xfl.wait_item(vector, b_xfl, cc)
                    d_yfl.wait_item(vector, b_yfl, cc)
                    d_out.wait_prev(vector, b_out, cc)
                    vector.tensor_tensor(out=outt[0][:], in0=y4res[cc % 3][:],
                                         in1=r4_b[:], op=ALU.mult)
                    vector.drain()
                    vector.tensor_tensor(out=outt[0][:], in0=outt[0][:],
                                         in1=nmr4_b[:], op=ALU.add)
                    vector.drain()
                    ins = vector.tensor_tensor(out=outt[0][:], in0=outt[0][:],
                                               in1=x2st_[cc % 3][:], op=ALU.add)
                    m_on.inc(ins, 1)

            @block.gpsimd
            def _(gpsimd):
                gpsimd.wait_ge(m_bc.h, bc4tar)
                for cc in range(1, CC, 2):
                    d_xfl.wait_item(gpsimd, b_xfl, cc)
                    d_yfl.wait_item(gpsimd, b_yfl, cc)
                    d_out.wait_prev(gpsimd, b_out, cc)
                    gpsimd.tensor_tensor(out=outt[1][:], in0=y4res[cc % 3][:],
                                         in1=r4_b[:], op=ALU.mult)
                    gpsimd.drain()
                    gpsimd.tensor_tensor(out=outt[1][:], in0=outt[1][:],
                                         in1=nmr4_b[:], op=ALU.add)
                    gpsimd.drain()
                    ins = gpsimd.tensor_tensor(out=outt[1][:], in0=outt[1][:],
                                               in1=x2st_[cc % 3][:],
                                               op=ALU.add)
                    m_onp.inc(ins, 1)

            @block.scalar
            def _(scalar):
                for cc in range(CC):
                    if cc % 2 == 0:
                        scalar.wait_ge(m_on.h, ON + cc // 2 + 1)
                    else:
                        scalar.wait_ge(m_onp.h, ONP + (cc + 1) // 2)
                    ins = scalar.dma_start(
                        out=out_fm[cc * 128:(cc + 1) * 128, q0:q0 + CQ],
                        in_=outt[cc % 2][:])
                    d_out.inc(ins, cc)

            if not last:
                b_wug = b_wug_next

        c.close()

        @block.sync
        def _(sync):
            d_out.wait_all(sync)

    return nc


def _make_consts():
    mask = np.zeros((128, 129), np.float32)
    for j in range(4):
        mask[32 * j:32 * (j + 1), 32 * j:32 * (j + 1)] = 1.0
    mask[:, 128] = 1.0
    p4 = np.zeros((4, 128), np.float32)
    for j in range(4):
        p4[j, 32 * j:32 * (j + 1)] = 1.0
    bm4 = np.zeros((128, 4), np.float32)
    for j in range(4):
        bm4[32 * j:32 * (j + 1), j] = 1.0
    return mask, p4, bm4


def make_in_maps(p, x, w_qkv, w_proj, b_proj, W_up, W_gate, W_down):
    xT = np.ascontiguousarray(np.transpose(x, (0, 2, 1)))
    wqkvT = np.ascontiguousarray(w_qkv.T)
    wprojT = np.ascontiguousarray(w_proj.T)
    bp_ = np.ascontiguousarray(b_proj.reshape(1, -1))
    wupT = np.ascontiguousarray(W_up.T)
    wgateT = np.ascontiguousarray(W_gate.T)
    wdownT = np.ascontiguousarray(W_down.T)
    mask, p4, bm4 = _make_consts()
    in_maps = []
    for k in range(p.n_cores):
        b_ = k // p.cores_per_b
        t0 = (k % p.cores_per_b) * p.T
        in_maps.append({
            "x_fm": np.ascontiguousarray(xT[b_, :, t0:t0 + p.T]),
            "wqkvT": wqkvT,
            "wprojT": wprojT,
            "bproj": bp_,
            "wup_sl": np.ascontiguousarray(wupT[k * p.RU:(k + 1) * p.RU]),
            "wgate_sl": np.ascontiguousarray(wgateT[k * p.RU:(k + 1) * p.RU]),
            "wdown_sl": np.ascontiguousarray(wdownT[k * p.RD:(k + 1) * p.RD]),
            "mask129": mask,
            "id128": np.eye(128, dtype=np.float32),
            "p4m": p4,
            "bm4": bm4,
        })
    return in_maps


def assemble_out(p, results):
    outT = np.empty((2, p.C, p.cores_per_b * p.T), np.float32)
    for k in range(p.n_cores):
        b_ = k // p.cores_per_b
        t0 = (k % p.cores_per_b) * p.T
        outT[b_, :, t0:t0 + p.T] = results[k]["out_fm"]
    return np.ascontiguousarray(np.transpose(outT, (0, 2, 1)))


def kernel(L0_attn_x, L0_attn_w_qkv, L0_attn_w_proj, L0_attn_b_proj,
           L0_ffn_W_up, L0_ffn_W_gate, L0_ffn_W_down):
    p = Params()
    nc = build_nc(p)
    in_maps = make_in_maps(p, L0_attn_x, L0_attn_w_qkv, L0_attn_w_proj,
                           L0_attn_b_proj, L0_ffn_W_up, L0_ffn_W_gate,
                           L0_ffn_W_down)
    res = run_bass_kernel_spmd(nc, in_maps, core_ids=list(range(p.n_cores)))
    return assemble_out(p, res.results)
